# version 06
# baseline (speedup 1.0000x reference)
import contextlib
import math

import numpy as np

import concourse.bass as bass
import concourse.mybir as mybir
from concourse.bass_utils import run_bass_kernel_spmd

F32 = mybir.dt.float32
BF16 = mybir.dt.bfloat16
AF = mybir.ActivationFunctionType
ALU = mybir.AluOpType
AX = mybir.AxisListType

P = 128
D = 4096
TC = 1024
NT = TC // P
DC = D // P
NH = 32
DK = 128
KC = 32
INTER = 11008
ICT = INTER // P
OS = 8
GROUPS = [18, 17, 17, 17, 17]
INV_SQRT_DK = 1.0 / math.sqrt(DK)
EPS = 1e-5

ENGS = ("sync", "tensor", "vector", "scalar", "gpsimd")


class Sem:

    def __init__(self, handle):
        self.h = handle
        self.n = 0

    def bump(self, k):
        self.n += k
        return self.n


def build_program(stop=None, dump=None):
    nc = bass.Bass("TRN2", target_bir_lowering=False)

    x_d = nc.dram_tensor("x", [TC, D], F32, kind="ExternalInput").ap()
    wq_d = nc.dram_tensor("wq", [NH, P, DC, P], F32, kind="ExternalInput").ap()
    wk_d = nc.dram_tensor("wk", [NH, P, DC, P], F32, kind="ExternalInput").ap()
    wv_d = nc.dram_tensor("wv", [NH, P, DC, P], F32, kind="ExternalInput").ap()
    wo_d = nc.dram_tensor("wo", [OS, P, NH, 512], F32, kind="ExternalInput").ap()
    gw_d = nc.dram_tensor("gw", [ICT, P, DC, P], F32, kind="ExternalInput").ap()
    uw_d = nc.dram_tensor("uw", [ICT, P, DC, P], F32, kind="ExternalInput").ap()
    dw_d = nc.dram_tensor("dw", [OS, P, ICT, 512], F32, kind="ExternalInput").ap()
    sc_d = nc.dram_tensor("sc", [1, 3], F32, kind="ExternalInput").ap()
    out_d = nc.dram_tensor("out", [TC, D], F32, kind="ExternalOutput").ap()
    dbg_d = nc.dram_tensor("dbg", [P, 16384], F32, kind="ExternalOutput").ap() \
        if dump else None
    dbg2_d = nc.dram_tensor("dbg2", [P, 16384], F32, kind="ExternalOutput").ap() \
        if dump else None

    k_own = nc.dram_tensor("k_own", [D, TC], BF16, kind="Internal").ap()
    v_own = nc.dram_tensor("v_own", [TC, D], BF16, kind="Internal").ap()
    k_all = nc.dram_tensor("k_all", [4 * D, TC], BF16, kind="Internal").ap()
    v_all = nc.dram_tensor("v_all", [4 * TC, D], BF16, kind="Internal").ap()
    k2o_d = nc.dram_tensor("k2o", [1, NH], F32, kind="Internal").ap()
    k2a_d = nc.dram_tensor("k2a", [4, NH], F32, kind="Internal").ap()
    x1d = nc.dram_tensor("x1d", [TC, D], F32, kind="Internal").ap()
    pa_d = nc.dram_tensor("pa", [TC, D], F32, kind="Internal").ap()
    pb_d = nc.dram_tensor("pb", [TC, D], F32, kind="Internal").ap()

    ctx = contextlib.ExitStack()

    ar1 = ctx.enter_context(nc.sbuf_tensor("ar1", [P, 16384], F32))
    ar2 = ctx.enter_context(nc.sbuf_tensor("ar2", [P, 16384], F32))
    ar3 = ctx.enter_context(nc.sbuf_tensor("ar3", [P, 11264], F32))
    ar4 = ctx.enter_context(nc.sbuf_tensor("ar4", [P, 8192], F32))

    ar1_16 = ar1[:].bitcast(BF16)
    ar2_16 = ar2[:].bitcast(BF16)
    ar3_16 = ar3[:].bitcast(BF16)
    ar4_16 = ar4[:].bitcast(BF16)

    lnx_fm = ar1_16
    qsq_v = ar1_16
    attn_fm = ar1_16
    ln2_fm = ar1_16
    ar1_3d = ar1_16.rearrange("p (k t) -> p k t", t=1024)

    q_fm = ar2_16
    wo_slot = [ar2_16[:, j * 8192:(j + 1) * 8192] for j in range(3)]
    h_fm = ar2_16

    xt = [ar3[:, i * 4096:(i + 1) * 4096] for i in range(2)]
    wslot = [ar3_16[:, j * 4096:(j + 1) * 4096] for j in range(3)]
    kst = [ar3_16[:, 12288 + i * 1024: 12288 + (i + 1) * 1024] for i in range(2)]
    vfm_sb = ar3_16[:, 14336:15360]
    vtm_sb = ar3_16[:, 15360:16384]
    ktile = [ar3_16[:, i * 4096:(i + 1) * 4096] for i in range(2)]
    vtile = [ar3_16[:, 8192 + i * 4096: 8192 + (i + 1) * 4096] for i in range(2)]
    pch = [ar3_16[:, 16384 + i * 1024: 16384 + (i + 1) * 1024] for i in range(4)]
    bc_sb = ar3[:, 10240:11264]
    xs = [ar3[:, i * 512:(i + 1) * 512] for i in range(2)]
    x1s = [ar3[:, 1024 + i * 512: 1024 + (i + 1) * 512] for i in range(2)]
    x1t = [ar3[:, i * 4096:(i + 1) * 4096] for i in range(2)]
    dwslot = [ar3_16[:, i * 9216:(i + 1) * 9216] for i in range(2)]
    prev_sb = [ar3[:, 9216 + i * 512: 9216 + (i + 1) * 512] for i in range(2)]
    outst = [ar3[:, 10240 + i * 512: 10240 + (i + 1) * 512] for i in range(2)]

    scr16 = ar4_16[:, 0:4096]
    lntm = [ar4_16[:, 4096 + i * 4096: 4096 + (i + 1) * 4096] for i in range(2)]
    qrow = ar4[0:1, 2048:3072]
    ksq = ar4_16[:, 0:1024]
    zacc = ar4[:, 0:1024]
    zaccB = ar4[:, 4096:5120]
    zacc16 = ar4_16[:, 2048:3072]
    attnU = [ar4_16[:, 3072 + i * 1024: 3072 + (i + 1) * 1024] for i in range(2)]
    srowL = ar4[0:1, 6144:7168]
    srowR = ar4[0:1, 7168:8192]
    gslot = [ar4_16[:, i * 4096:(i + 1) * 4096] for i in range(2)]
    uslot = [ar4_16[:, 8192 + i * 4096: 8192 + (i + 1) * 4096] for i in range(2)]

    ident = ctx.enter_context(nc.sbuf_tensor("ident", [P, P], BF16))
    ones_col = ctx.enter_context(nc.sbuf_tensor("ones_col", [P, 1], BF16))
    ones_row = ctx.enter_context(nc.sbuf_tensor("ones_row", [1, P], F32))
    sumx = ctx.enter_context(nc.sbuf_tensor("sumx", [P, 1], F32))
    ssq = ctx.enter_context(nc.sbuf_tensor("ssq", [P, 1], F32))
    m128 = ctx.enter_context(nc.sbuf_tensor("m128", [P, 1], F32))
    std128 = ctx.enter_context(nc.sbuf_tensor("std128", [P, 1], F32))
    rstd128 = ctx.enter_context(nc.sbuf_tensor("rstd128", [P, 1], F32))
    gs128 = ctx.enter_context(nc.sbuf_tensor("gs128", [P, 1], F32))
    us128 = ctx.enter_context(nc.sbuf_tensor("us128", [P, 1], F32))
    ds128 = ctx.enter_context(nc.sbuf_tensor("ds128", [P, 1], F32))
    sc_sb = ctx.enter_context(nc.sbuf_tensor("sc_sb", [1, 3], F32))
    qn2max = ctx.enter_context(nc.sbuf_tensor("qn2max", [1, NH], F32))
    k2row = ar4[0:1, 6144:6272]
    k2all = ar4[0:1, 6272:6304]
    prodv = ar4[0:1, 6304:6336]
    biasv = ar4[0:1, 6336:6368]
    k2own = ar4[0:1, 6368:6400]
    bias_all = ar4[:, 2560:2592]
    sg_sb = ctx.enter_context(nc.sbuf_tensor("sg_sb", [P, 1024], BF16))
    eps128 = ctx.enter_context(nc.sbuf_tensor("eps128", [P, 1], F32))

    psh = ctx.enter_context(nc.psum_tensor("ps", [P, 4096], F32))
    psf = psh[:]
    ps16 = psf.bitcast(BF16)

    def bank(k):
        return psf[:, k * 512:(k + 1) * 512]

    def pair(k):
        return psf[:, k * 512:(k + 2) * 512]

    tr_p = [ps16[:, i * 1024:(i + 1) * 1024] for i in range(2)]
    ps_qn = psf[0:1, 2560:3584]
    ps_kn = [psf[0:1, 2048 + i * 512: 2048 + (i + 1) * 512] for i in range(2)]
    ps_sums = psf[0:1, 3072:4096]
    ps_bias = psf[:, 3584:3584 + NH]
    ps_scb = [psf[:, 3584 + j: 3585 + j] for j in range(3)]
    ps_vt = ps16[:, 4096:5120]

    all_sems = []

    def mksem(name):
        s = Sem(ctx.enter_context(nc.semaphore(name)))
        all_sems.append(s)
        return s

    def mkstream(name, n):
        return [mksem(f"{name}{i}") for i in range(n)]

    PE = mksem("PE")
    DV = mksem("DV")
    AC = mksem("AC")
    CC = mksem("CC")
    PL = mksem("PL")
    sSC = mksem("sSC")
    sX = mkstream("sX", 2)
    sW = mkstream("sW", 3)
    sKST = mkstream("sKST", 2)
    sVOWN = mksem("sVOWN")
    sK2 = mksem("sK2")
    sKL = mkstream("sKL", 2)
    sVL = mkstream("sVL", 2)
    sWO = mkstream("sWO", 3)
    sXS = mkstream("sXS", 2)
    sX1W = mkstream("sX1W", 2)
    sX1R = mkstream("sX1R", 2)
    sG = mkstream("sG", 2)
    sU = mkstream("sU", 2)
    sDW = mkstream("sDW", 2)
    sPREV = mkstream("sPREV", 2)
    sOUT = mkstream("sOUT", 2)
    sDBG = mksem("sDBG")

    steps = {e: [] for e in ENGS}
    waited = {e: {} for e in ENGS}

    def st(eng, fn, waits=(), inc=None, inck=1):
        w = []
        for (s, v) in waits:
            if not v or v <= 0:
                continue
            if waited[eng].get(s, 0) >= v:
                continue
            waited[eng][s] = v
            w.append((s, v))
        w = tuple(w)
        val = inc.bump(inck) if inc is not None else None
        steps[eng].append((w, fn, (inc.h if inc else None), inck))
        return val

    def dma(eng, fn, sem, waits=()):
        v = st(eng, fn, waits=waits, inc=sem, inck=16)
        return (sem, v)

    marks = {}

    def mark(name):
        marks[name] = {e: len(steps[e]) for e in ENGS}

    def maybe_dump(name, src_ap, second=None):
        if dump != name:
            return
        waits = [(S, S.n) for S in all_sems]

        def f(s, src_ap=src_ap, second=second):
            for i in range(8):
                s.dma_start(out=dbg_d[:, i * 2048:(i + 1) * 2048],
                            in_=src_ap[:, i * 2048:(i + 1) * 2048]
                            ).then_inc(sDBG.h, 16)
            if second is not None:
                for i in range(8):
                    s.dma_start(out=dbg2_d[:, i * 2048:(i + 1) * 2048],
                                in_=second[:, i * 2048:(i + 1) * 2048]
                                ).then_inc(sDBG.h, 16)
            return None
        st("sync", f, waits=waits)
        sDBG.bump(256 if second is not None else 128)

    pl_id0 = st("gpsimd", lambda g: g.memset(ident[:], 0.0), inc=PL)
    st("gpsimd",
       lambda g: g.affine_select(out=ident[:], in_=ident[:],
                                 compare_op=ALU.not_equal, fill=1.0, base=0,
                                 pattern=[[-1, P]], channel_multiplier=1),
       waits=((PL, pl_id0),), inc=PL)

    def setup_pool(g):
        g.memset(ones_col[:], 1.0)
        g.memset(eps128[:], EPS)
        return g.memset(ones_row[:], 1.0)
    pl_setup = st("gpsimd", setup_pool, inc=PL)

    t_sc = dma("sync", lambda s: s.dma_start(out=sc_sb[:], in_=sc_d[:]), sSC)

    def setup_bcast(t):
        last = None
        for j in range(3):
            last = t.matmul(ps_scb[j], ones_row[:], sc_sb[0:1, j:j + 1],
                            start=True, stop=True, skip_group_check=True)
        return last
    pe_scb = st("tensor", setup_bcast,
                waits=((PL, pl_setup), t_sc), inc=PE)

    def setup_copy(v):
        v.tensor_copy(out=gs128[:], in_=ps_scb[0])
        v.tensor_copy(out=us128[:], in_=ps_scb[1])
        return v.tensor_copy(out=ds128[:], in_=ps_scb[2])
    dv_scales = st("vector", setup_copy, waits=((PE, pe_scb),), inc=DV)

    mark("setup")

    L = {"pe_tr": {}, "dv_evac": {}, "dv_apply": {}, "ac_sq": {}}

    def emit_ln_tile(tag, i, src_tile, act_waits=()):
        sl = i % 2
        ac1 = st("scalar",
                 lambda sc, src=src_tile: sc.activation(
                     out=scr16, in_=src, func=AF.Copy, accum_out=sumx[:]),
                 waits=act_waits, inc=AC)
        dv1 = st("vector",
                 lambda v: v.tensor_scalar(out=m128[:], in0=sumx[:],
                                           scalar1=-1.0 / D, scalar2=None,
                                           op0=ALU.mult),
                 waits=((AC, ac1),), inc=DV)
        ac2a = st("scalar",
                  lambda sc, src=src_tile: sc.activation(
                      out=scr16, in_=src, func=AF.Square,
                      bias=m128[:], scale=1.0, accum_out=ssq[:]),
                  waits=((DV, dv1),), inc=AC)
        ac2 = st("scalar",
                 lambda sc: sc.activation(out=std128[:], in_=ssq[:],
                                          func=AF.Sqrt, scale=1.0 / D,
                                          bias=eps128[:]),
                 waits=((AC, ac2a),), inc=AC)
        L["ac_sq"][(tag, i)] = ac2
        dv2 = st("vector", lambda v: v.reciprocal(out=rstd128[:], in_=std128[:]),
                 waits=((AC, ac2),), inc=DV)
        war = L["pe_tr"].get((tag, i - 2, 3), 0)
        dv3 = st("vector",
                 lambda v, src=src_tile, sl=sl: v.tensor_scalar(
                     out=lntm[sl], in0=src, scalar1=m128[:], scalar2=rstd128[:],
                     op0=ALU.add, op1=ALU.mult),
                 waits=((DV, dv2), (PE, war)), inc=DV)
        L["dv_apply"][(tag, i)] = dv3
        return dv3

    def emit_ln_transposes(tag, i):
        sl = i % 2
        dv3 = L["dv_apply"][(tag, i)]
        for b in range(4):
            par = b % 2
            war = max(L["dv_evac"].get((tag, i, b - 2), 0),
                      L["dv_evac"].get((tag, i - 1, b + 2), 0) if b < 2 else 0)

            def tr(t, b=b, par=par, sl=sl):
                last = None
                for j in range(8):
                    k = b * 8 + j
                    last = t.transpose(tr_p[par][:, j * 128:(j + 1) * 128],
                                       lntm[sl][:, k * 128:(k + 1) * 128],
                                       ident[:])
                return last
            pe1 = st("tensor", tr,
                     waits=((DV, dv3), (DV, war), (PL, pl_setup)), inc=PE)
            L["pe_tr"][(tag, i, b)] = pe1

            def evac(v, b=b, par=par, i=i):
                src = tr_p[par].rearrange("p (j q) -> p j q", q=128)
                dst = ar1_3d[:, b * 8:(b + 1) * 8, i * 128:(i + 1) * 128]
                return v.tensor_copy(out=dst, in_=src)
            L["dv_evac"][(tag, i, b)] = st("vector", evac,
                                           waits=((PE, pe1),), inc=DV)
        return L["dv_evac"][(tag, i, 3)]

    t_x = {}
    for i in range(NT):
        sl = i % 2
        war = L["dv_apply"].get(("A", i - 2), 0)
        t_x[i] = dma("sync",
                     lambda s, i=i, sl=sl: s.dma_start(
                         out=xt[sl], in_=x_d[i * P:(i + 1) * P, :]),
                     sX[sl], waits=((DV, war),))
        emit_ln_tile("A", i, xt[sl], act_waits=(t_x[i],))
        emit_ln_transposes("A", i)
    dv_lnx_done = L["dv_evac"][("A", NT - 1, 3)]
    maybe_dump("A", ar1[:])
    mark("A")

    t_w = {}
    pe_mm_done = {}
    dv_kst = {}
    t_kown = {}
    dv_vfm = {}
    pe_vt = {}
    dv_vtm = {}
    t_vown = {}
    dv_q = {}

    def w_dma(src, oc, slot):
        def f(g, src=src, oc=oc, slot=slot):
            return g.dma_start(
                out=wslot[slot].rearrange("p (dc ow) -> p dc ow", ow=P),
                in_=src[oc])
        return f

    def proj_matmuls(t, slot, dst_pair):
        last = None
        for dc in range(DC):
            lhsT = wslot[slot][:, dc * 128:(dc + 1) * 128]
            for half in range(2):
                last = t.matmul(
                    dst_pair[:, half * 512:(half + 1) * 512],
                    lhsT,
                    lnx_fm[:, dc * 1024 + half * 512: dc * 1024 + (half + 1) * 512],
                    start=(dc == 0), stop=(dc == DC - 1),
                    skip_group_check=True)
        return last

    ac_ksq = {}
    kn_pe = {}
    dv_k2 = {}

    def emit_kn(oc):
        def kn_mm(t):
            last = None
            for j in range(2):
                last = t.matmul(ps_kn[j], ones_col[:],
                                ksq[:, j * 512:(j + 1) * 512],
                                start=True, stop=True, skip_group_check=True)
            return last
        kn_pe[oc] = st("tensor", kn_mm,
                       waits=((AC, ac_ksq[oc]), (DV, dv_k2.get(oc - 1, 0))),
                       inc=PE)
        dv_k2[oc] = st("vector",
                       lambda v, oc=oc: v.tensor_reduce(
                           out=k2own[0:1, oc:oc + 1], in_=psf[0:1, 2048:3072],
                           axis=AX.X, op=ALU.max),
                       waits=((PE, kn_pe[oc]),), inc=DV)

    for oc in range(NH):
        widx = oc
        slot = widx % 3
        kw = [(PE, pe_mm_done.get(widx - 3, 0))]
        if widx < 3:
            kw.append((DV, L["dv_apply"][("A", NT - 1)]))
            kw.append((AC, L["ac_sq"][("A", NT - 1)]))
        t_w[widx] = dma("gpsimd", w_dma(wk_d, oc, slot), sW[slot], waits=kw)
        pe_mm_done[widx] = st(
            "tensor",
            lambda t, slot=slot, oc=oc: proj_matmuls(t, slot, pair((oc % 2) * 2)),
            waits=(t_w[widx], (DV, dv_lnx_done), (DV, dv_kst.get(oc - 2, 0))),
            inc=PE)
        if oc >= 1:
            emit_kn(oc - 1)
        dv_kst[oc] = st(
            "vector",
            lambda v, oc=oc: v.tensor_copy(out=kst[oc % 2], in_=pair((oc % 2) * 2)),
            waits=((PE, pe_mm_done[widx]),
                   t_kown.get(oc - 2, (CC, 0)),
                   (AC, ac_ksq.get(oc - 2, 0))), inc=DV)
        t_kown[oc] = dma(
            "sync",
            lambda s, oc=oc: s.dma_start(
                out=k_own[oc * 128:(oc + 1) * 128, :], in_=kst[oc % 2]),
            sKST[oc % 2], waits=((DV, dv_kst[oc]),))
        ac_ksq[oc] = st("scalar",
                        lambda sc, oc=oc: sc.activation(
                            out=ksq, in_=kst[oc % 2], func=AF.Square),
                        waits=((DV, dv_kst[oc]), (PE, kn_pe.get(oc - 1, 0))),
                        inc=AC)
    emit_kn(NH - 1)
    t_k2w = dma("sync",
                lambda s: s.dma_start(out=k2o_d[:], in_=k2own[0:1, :]),
                sK2, waits=((DV, dv_k2[NH - 1]),))

    rg = [[0, 1, 2, 3], [4, 5, 6, 7]]
    cc_k2 = cc_k = None
    for oc in range(NH):
        widx = NH + oc
        slot = widx % 3
        t_w[widx] = dma("gpsimd", w_dma(wv_d, oc, slot), sW[slot],
                        waits=((PE, pe_mm_done.get(widx - 3, 0)),))
        if oc == 2:
            cc_k2 = st("gpsimd",
                       lambda g: g.collective_compute(
                           "AllGather", ALU.bypass, replica_groups=rg,
                           ins=[k2o_d.opt()], outs=[k2a_d.opt()]),
                       waits=(t_k2w,), inc=CC)
            cc_k = st("gpsimd",
                      lambda g: g.collective_compute(
                          "AllGather", ALU.bypass, replica_groups=rg,
                          ins=[k_own.opt()], outs=[k_all.opt()]),
                      waits=((sKST[0], sKST[0].n), (sKST[1], sKST[1].n)),
                      inc=CC)
        mm_w = [t_w[widx], (DV, dv_vfm.get(oc - 2, 0))]
        if oc < 2:
            mm_w.append((DV, dv_kst[NH - 1]))
        pe_mm_done[widx] = st(
            "tensor",
            lambda t, slot=slot, oc=oc: proj_matmuls(t, slot, pair((oc % 2) * 2)),
            waits=mm_w, inc=PE)
        dv_vfm[oc] = st(
            "vector",
            lambda v, oc=oc: v.tensor_copy(out=vfm_sb, in_=pair((oc % 2) * 2)),
            waits=((PE, pe_mm_done[widx]), (PE, pe_vt.get(oc - 1, 0))), inc=DV)

        def vtr(t, oc=oc):
            last = None
            for j in range(8):
                last = t.transpose(ps_vt[:, j * 128:(j + 1) * 128],
                                   vfm_sb[:, j * 128:(j + 1) * 128], ident[:])
            return last
        pe_vt[oc] = st("tensor", vtr,
                       waits=((DV, dv_vfm[oc]), (DV, dv_vtm.get(oc - 1, 0))),
                       inc=PE)
        dv_vtm[oc] = st(
            "vector",
            lambda v: v.tensor_copy(out=vtm_sb, in_=ps_vt),
            waits=((PE, pe_vt[oc]), t_vown.get(oc - 1, (CC, 0))), inc=DV)
        t_vown[oc] = dma(
            "sync",
            lambda s, oc=oc: s.dma_start(
                out=v_own[:, oc * 128:(oc + 1) * 128].rearrange(
                    "(b p) o -> p b o", p=P),
                in_=vtm_sb.rearrange("p (b o) -> p b o", o=P)),
            sVOWN, waits=((DV, dv_vtm[oc]),))

    cc_v = st("gpsimd",
              lambda g: g.collective_compute(
                  "AllGather", ALU.bypass, replica_groups=rg,
                  ins=[v_own.opt()], outs=[v_all.opt()]),
              waits=((sVOWN, sVOWN.n),), inc=CC)

    for oc in range(NH):
        widx = 2 * NH + oc
        slot = widx % 3
        t_w[widx] = dma("gpsimd", w_dma(wq_d, oc, slot), sW[slot],
                        waits=((PE, pe_mm_done.get(widx - 3, 0)),))
        mm_w = [t_w[widx], (DV, dv_q.get(oc - 2, 0))]
        if oc < 2:
            mm_w.append((DV, dv_vfm[NH - 1]))
        pe_mm_done[widx] = st(
            "tensor",
            lambda t, slot=slot, oc=oc: proj_matmuls(t, slot, pair((oc % 2) * 2)),
            waits=mm_w, inc=PE)
        dv_q[oc] = st(
            "vector",
            lambda v, oc=oc: v.tensor_copy(
                out=q_fm[:, oc * 1024:(oc + 1) * 1024], in_=pair((oc % 2) * 2)),
            waits=((PE, pe_mm_done[widx]),), inc=DV)
    pe_q_done = pe_mm_done[3 * NH - 1]

    ac_qsq = st("scalar",
                lambda sc: sc.activation(out=qsq_v, in_=q_fm, func=AF.Square),
                waits=((DV, dv_q[NH - 1]),), inc=AC)
    pe_qn = {}
    dv_qrow = {}
    dv_qn = {}
    for h in range(NH):
        def qn_mm(t, h=h):
            last = None
            for half in range(2):
                last = t.matmul(
                    ps_qn[:, half * 512:(half + 1) * 512], ones_col[:],
                    qsq_v[:, h * 1024 + half * 512: h * 1024 + (half + 1) * 512],
                    start=True, stop=True, skip_group_check=True)
            return last
        pe_qn[h] = st("tensor", qn_mm,
                      waits=((AC, ac_qsq), (DV, dv_qrow.get(h - 1, 0))), inc=PE)
        dv_qrow[h] = st("vector",
                        lambda v: v.tensor_copy(out=qrow, in_=ps_qn),
                        waits=((PE, pe_qn[h]), (DV, dv_qn.get(h - 1, 0))), inc=DV)
        dv_qn[h] = st("vector",
                      lambda v, h=h: v.tensor_reduce(
                          out=qn2max[0:1, h:h + 1], in_=qrow, axis=AX.X,
                          op=ALU.max),
                      waits=((DV, dv_qrow[h]),), inc=DV)

    t_k2r = dma("sync",
                lambda s: s.dma_start(out=k2row,
                                      in_=k2a_d.rearrange("c h -> (c h)")),
                sK2, waits=((CC, cc_k2),))
    dv_k2c = st("vector",
                lambda v: v.tensor_tensor(out=k2all, in0=k2row[0:1, 0:NH],
                                          in1=k2row[0:1, NH:2 * NH],
                                          op=ALU.max),
                waits=(t_k2r,), inc=DV)
    dv_k2c = st("vector",
                lambda v: v.tensor_tensor(out=k2all, in0=k2all,
                                          in1=k2row[0:1, 2 * NH:3 * NH],
                                          op=ALU.max),
                waits=((DV, dv_k2c),), inc=DV)
    dv_k2c = st("vector",
                lambda v: v.tensor_tensor(out=k2all, in0=k2all,
                                          in1=k2row[0:1, 3 * NH:4 * NH],
                                          op=ALU.max),
                waits=((DV, dv_k2c),), inc=DV)
    dv_prod = st("vector",
                 lambda v: v.tensor_tensor(out=prodv, in0=qn2max[:],
                                           in1=k2all, op=ALU.mult),
                 waits=((DV, dv_k2c), (DV, dv_qn[NH - 1])), inc=DV)
    ac_biasv = st("scalar",
                  lambda sc: sc.activation(out=biasv, in_=prodv,
                                           func=AF.Sqrt, scale=1.0 / DK),
                  waits=((DV, dv_prod),), inc=AC)
    pe_bcb = st("tensor",
                lambda t: t.matmul(ps_bias, ones_row[:], biasv,
                                   start=True, stop=True,
                                   skip_group_check=True),
                waits=((AC, ac_biasv), (DV, dv_scales)), inc=PE)
    dv_biasall = st("vector",
                    lambda v: v.tensor_scalar(out=bias_all, in0=ps_bias,
                                              scalar1=-1.0, scalar2=None,
                                              op0=ALU.mult),
                    waits=((PE, pe_bcb),), inc=DV)
    maybe_dump("B", ar2[:], second=ar1[:])
    mark("B")
    mark("Q")

    t_k = {}
    t_v = {}
    dv_attn = {}
    dv_bc = {}
    pe_st_last = {}
    pe_pv_last = {}
    pe_sums_last = {}
    pe_bc = {}
    dv_evacU = {}
    dv_srow = {}
    ac_rec = {}
    ac_exp_l = {}
    dvD = [0]
    plD = [PL.n]

    t_wo = {}

    def emit_wo(widx, extra_waits=()):
        os_, half = widx // 2, widx % 2
        slot = widx % 3
        t_wo[widx] = dma(
            "gpsimd",
            lambda g, os_=os_, half=half, slot=slot: g.dma_start(
                out=wo_slot[slot].rearrange("p (oc ow) -> p oc ow", ow=512),
                in_=wo_d[os_, :, half * 16:(half + 1) * 16, :]),
            sWO[slot], waits=extra_waits)

    def dvst(fn, waits=(), ):
        v = st("vector", fn, waits=tuple(waits) + ((DV, dvD[0]),), inc=DV)
        dvD[0] = v
        return v

    def emit_psums(cp, h, psums, exp_ac):
        w = [(AC, exp_ac[cp])]
        if cp == 3:
            if h == 0:
                w += [(DV, dv_qrow[NH - 1]), (DV, dv_qn[NH - 1]),
                      (DV, dv_biasall)]
            else:
                w.append((DV, dv_bc[h - 1]))

        def sums_mm(t, cp=cp):
            last = None
            for j in range(2):
                last = t.matmul(ps_sums[:, j * 512:(j + 1) * 512], ones_col[:],
                                pch[3][:, j * 512:(j + 1) * 512],
                                start=(cp == 3), stop=False,
                                skip_group_check=True)
            return last
        psums[cp] = st("tensor", sums_mm, waits=w, inc=PE)

    def emit_merge(h):
        def merge_mm(t):
            last = None
            for j in range(2):
                last = t.matmul(ps_sums[:, j * 512:(j + 1) * 512], ones_col[:],
                                zacc16[:, j * 512:(j + 1) * 512],
                                start=False, stop=True, skip_group_check=True)
            return last
        pe_sums_last[h] = st("tensor", merge_mm,
                             waits=((DV, zcast[h]),), inc=PE)

    def emit_tail_dv(h):
        dv_bc[h] = dvst(lambda v: v.tensor_copy(out=bc_sb, in_=psf[:, 3072:4096]),
                        waits=((PE, pe_bc[h]),))
        dv_attn[h] = dvst(
            lambda v, h=h: v.tensor_tensor(
                out=attn_fm[:, h * 1024:(h + 1) * 1024],
                in0=attnU[h % 2], in1=bc_sb, op=ALU.mult))

    def emit_tail_ac(h):
        st("scalar", lambda sc: sc.activation(out=srowR, in_=srowL, func=AF.Ln),
           waits=((DV, dv_srow[h]),), inc=AC)
        ac_rec[h] = st("scalar",
                       lambda sc: sc.activation(out=srowL, in_=srowR,
                                                func=AF.Exp, scale=-1.0),
                       inc=AC)

    def emit_bc(h):
        def bc_mm(t):
            last = None
            for j in range(2):
                last = t.matmul(psf[:, 3072 + j * 512: 3072 + (j + 1) * 512],
                                ones_row[:], srowL[0:1, j * 512:(j + 1) * 512],
                                start=True, stop=True, skip_group_check=True)
            return last
        pe_bc[h] = st("tensor", bc_mm,
                      waits=((AC, ac_rec[h]), (DV, dv_srow[h])), inc=PE)

    padd_last = {}
    zcast = {}
    for h in range(NH):
        sl = h % 2

        def kload(s, h=h, sl=sl):
            for j in range(4):
                s.dma_start(
                    out=ktile[sl][:, j * 1024:(j + 1) * 1024],
                    in_=k_all[j * D + h * 128: j * D + (h + 1) * 128, :]
                ).then_inc(sKL[sl].h, 16)
            return None
        klw = [(CC, cc_k), (PE, pe_st_last.get(h - 2, 0))]
        if h < 2:
            klw.append((PE, pe_q_done))
        st("sync", kload, waits=klw)
        t_k[h] = (sKL[sl], sKL[sl].bump(64))

        def vload(s, h=h, sl=sl):
            for half in range(2):
                s.dma_start(
                    out=vtile[sl][:, half * 2048:(half + 1) * 2048].rearrange(
                        "p (kc dv) -> p kc dv", dv=128),
                    in_=v_all[half * 2048:(half + 1) * 2048,
                              h * 128:(h + 1) * 128].rearrange(
                        "(kc p) dv -> p kc dv", p=P)
                ).then_inc(sVL[sl].h, 16)
            return None
        vlw = [(CC, cc_v), (PE, pe_pv_last.get(h - 2, 0))]
        if h < 2:
            vlw.append((PE, pe_q_done))
            vlw.append((DV, dv_vtm[NH - 1]))
            vlw.append((AC, ac_ksq[NH - 1]))
        st("sync", vload, waits=vlw)
        t_v[h] = (sVL[sl], sVL[sl].bump(32))

        st_pe = {}
        exp_ac = {}
        pv_pe = {}
        zadd = {}
        prev_exp = ac_exp_l.get(h - 1, {})

        def emit_st(c):
            w = [t_k[h]]
            if c >= 2:
                w.append((AC, exp_ac[c - 2]))
            elif h > 0:
                w.append((AC, prev_exp[30 + c]))
            elif c == 0:
                w.append((DV, dv_q[NH - 1]))

            def st_mm(t, c=c, sl=sl, h=h):
                last = None
                for half in range(2):
                    last = t.matmul(
                        pair((c % 2) * 2)[:, half * 512:(half + 1) * 512],
                        ktile[sl][:, c * 128:(c + 1) * 128],
                        q_fm[:, h * 1024 + half * 512: h * 1024 + (half + 1) * 512],
                        start=True, stop=True, skip_group_check=True)
                return last
            st_pe[c] = st("tensor", st_mm, waits=w, inc=PE)

        def emit_pv(c):
            w = [(AC, exp_ac[c]), t_v[h]]
            if c == 0:
                if h > 0:
                    w.append((DV, dv_evacU[h - 1]))
                else:
                    w += [(DV, dv_vtm[NH - 1]), (DV, dv_k2[NH - 1]),
                          (DV, dv_qrow[NH - 1])]

            def pv_mm(t, c=c, sl=sl):
                last = None
                for half in range(2):
                    last = t.matmul(
                        pair(4)[:, half * 512:(half + 1) * 512],
                        vtile[sl][:, c * 128:(c + 1) * 128],
                        pch[c % 4][:, half * 512:(half + 1) * 512],
                        start=(c == 0), stop=(c == KC - 1),
                        skip_group_check=True)
                return last
            pv_pe[c] = st("tensor", pv_mm, waits=w, inc=PE)

        psums = {}
        padd = {}
        for c in range(KC):
            if c >= 2:
                emit_pv(c - 2)
            emit_st(c)
            if c == 3 and h > 0:
                emit_merge(h - 1)
            if c == 6 and h > 0:
                emit_bc(h - 1)
            if c >= 7 and c % 4 == 3:
                emit_psums(c - 4, h, psums, exp_ac)

            if c == 4 and h > 0:
                emit_tail_ac(h - 1)
            w = [(PE, st_pe[c])]
            if c >= 4:
                w.append(((PE, psums[c - 4]) if (c - 4) % 4 == 3
                          else (PL, padd[c - 4])))
            elif h > 0:
                w.append((PE, pe_pv_last[h - 1]))
                w.append((PL, padd_last[h - 1]))
            if h == 0 and c == 0:
                w.append((DV, dv_biasall))
            exp_ac[c] = st("scalar",
                           lambda sc, c=c, h=h: sc.activation(
                               out=pch[c % 4], in_=pair((c % 2) * 2),
                               func=AF.Exp, bias=bias_all[:, h:h + 1],
                               scale=INV_SQRT_DK),
                           waits=w, inc=AC)

            if c == 3 and h > 0:
                dv_srow[h - 1] = dvst(
                    lambda v: v.tensor_copy(out=srowL, in_=ps_sums),
                    waits=((PE, pe_sums_last[h - 1]),))
            if c == 6 and h > 0:
                emit_tail_dv(h - 1)
            if c % 4 != 3:
                pw = [(AC, exp_ac[c]), (PL, plD[0])]
                if c == 0:
                    if h > 0:
                        pw.append((DV, zcast[h - 1]))
                    padd[c] = st(
                        "gpsimd",
                        lambda g, c=c: g.tensor_copy(out=zaccB, in_=pch[c % 4]),
                        waits=pw, inc=PL)
                else:
                    padd[c] = st(
                        "gpsimd",
                        lambda g, c=c: g.tensor_tensor(out=zaccB, in0=zaccB,
                                                       in1=pch[c % 4], op=ALU.add),
                        waits=pw, inc=PL)
                plD[0] = padd[c]
        padd_last[h] = padd[KC - 2]

        emit_pv(KC - 2)
        emit_pv(KC - 1)
        emit_psums(KC - 1, h, psums, exp_ac)
        pe_st_last[h] = st_pe[KC - 1]
        pe_pv_last[h] = pv_pe[KC - 1]
        ac_exp_l[h] = exp_ac

        zcast[h] = dvst(
            lambda v: v.tensor_copy(out=zacc16, in_=zaccB),
            waits=((PL, padd[KC - 2]), (PE, pe_sums_last.get(h - 1, 0))))
        dv_evacU[h] = dvst(
            lambda v, h=h: v.tensor_copy(out=attnU[h % 2], in_=pair(4)),
            waits=((PE, pe_pv_last[h]),))
        if h in (7, 15, 23):
            emit_wo((h - 7) // 8, extra_waits=((PE, pe_st_last[h]),))

    emit_merge(NH - 1)
    dv_srow[NH - 1] = dvst(
        lambda v: v.tensor_copy(out=srowL, in_=ps_sums),
        waits=((PE, pe_sums_last[NH - 1]),))
    emit_tail_ac(NH - 1)
    emit_bc(NH - 1)
    emit_tail_dv(NH - 1)

    ac_rec_last = AC.n
    maybe_dump("D", ar1[:])
    mark("D")

    pe_o = {}
    t_xs = {}
    dv_x1 = {}
    t_x1w = {}

    peE0 = PE.n
    for widx in range(3, 2 * OS):
        war = peE0 + ((widx - 3) // 2 + 1) * NT
        emit_wo(widx, extra_waits=((PE, war),))

    def eprev(os_, tc, k):
        i = os_ * NT + tc - k
        return (i // NT, i % NT) if i >= 0 else None

    for os_ in range(OS):
        for tc in range(NT):
            par = (os_ * NT + tc) % 2
            p2 = eprev(os_, tc, 2)
            w = [t_wo[os_ * 2], t_wo[os_ * 2 + 1],
                 (DV, dv_attn[NH - 1]),
                 (DV, dv_x1.get(p2, 0))]

            def o_mm(t, os_=os_, tc=tc, par=par):
                last = None
                for oc in range(NH):
                    slot = (os_ * 2 + oc // 16) % 3
                    last = t.matmul(
                        bank(par),
                        attn_fm[:, oc * 1024 + tc * 128: oc * 1024 + (tc + 1) * 128],
                        wo_slot[slot][:, (oc % 16) * 512:((oc % 16) + 1) * 512],
                        start=(oc == 0), stop=(oc == NH - 1),
                        skip_group_check=True)
                return last
            pe_o[(os_, tc)] = st("tensor", o_mm, waits=w, inc=PE)
            assert pe_o[(os_, tc)] == peE0 + os_ * NT + tc + 1

            xw = [(DV, dv_x1.get(p2, 0))]
            if os_ * NT + tc < 2:
                xw.append((PE, pe_sums_last[NH - 1]))
                xw.append((DV, dv_attn[NH - 1]))
            t_xs[(os_, tc)] = dma(
                "sync",
                lambda s, os_=os_, tc=tc, par=par: s.dma_start(
                    out=xs[par],
                    in_=x_d[tc * 128:(tc + 1) * 128, os_ * 512:(os_ + 1) * 512]),
                sXS[par], waits=xw)
            dv_x1[(os_, tc)] = st(
                "vector",
                lambda v, par=par: v.tensor_tensor(
                    out=x1s[par], in0=bank(par), in1=xs[par], op=ALU.add),
                waits=((PE, pe_o[(os_, tc)]), t_xs[(os_, tc)],
                       t_x1w.get(p2, (CC, 0))), inc=DV)
            t_x1w[(os_, tc)] = dma(
                "sync",
                lambda s, os_=os_, tc=tc, par=par: s.dma_start(
                    out=x1d[tc * 128:(tc + 1) * 128, os_ * 512:(os_ + 1) * 512],
                    in_=x1s[par]),
                sX1W[par], waits=((DV, dv_x1[(os_, tc)]),))

    pe_attn_free = pe_o[(OS - 1, NT - 1)]
    mark("E")

    t_x1r = {}
    for i in range(NT):
        sl = i % 2
        rw = [(sX1W[0], sX1W[0].n), (sX1W[1], sX1W[1].n),
              (DV, L["dv_apply"].get(("E2", i - 2), 0))]
        t_x1r[i] = dma("sync",
                       lambda s, i=i, sl=sl: s.dma_start(
                           out=x1t[sl], in_=x1d[i * P:(i + 1) * P, :]),
                       sX1R[sl], waits=rw)
        extra = [t_x1r[i]]
        if i == 0:
            extra += [(PE, pe_attn_free), (AC, ac_rec_last)]
        emit_ln_tile("E2", i, x1t[sl], act_waits=tuple(extra))
        emit_ln_transposes("E2", i)
    dv_ln2_done = L["dv_evac"][("E2", NT - 1, 3)]
    ac_e2_last = L["ac_sq"][("E2", NT - 1)]
    maybe_dump("E2", ar1[:])
    mark("E2")

    t_g = {}
    t_u = {}
    pe_gate = {}
    pe_up = {}
    ac_silu = {}
    dv_h = {}
    pe_dmm = {}
    dv_out = {}
    t_outw = {}
    t_prev = {}

    def fprev(g_i, os_, tc, k):
        i = (g_i * OS + os_) * NT + tc - k
        if i < 0:
            return None
        t = i % NT
        i //= NT
        return (i // OS, i % OS, t)

    gbase = 0
    for g_i, gsz in enumerate(GROUPS):
        ic_last = gbase + gsz - 1
        for icg in range(gsz):
            ic = gbase + icg
            sl2 = ic % 2
            gw_w = [(PE, pe_gate.get(ic - 2, 0))]
            uw_w = [(PE, pe_up.get(ic - 2, 0))]
            if ic == 0:
                gw_w += [(DV, dv_ln2_done), (AC, ac_e2_last)]
                uw_w += [(DV, dv_ln2_done), (AC, ac_e2_last)]
            t_g[ic] = dma("gpsimd",
                          lambda g, ic=ic, sl2=sl2: g.dma_start(
                              out=gslot[sl2].rearrange("p (dc iw) -> p dc iw", iw=P),
                              in_=gw_d[ic]),
                          sG[sl2], waits=gw_w)
            t_u[ic] = dma("gpsimd",
                          lambda g, ic=ic, sl2=sl2: g.dma_start(
                              out=uslot[sl2].rearrange("p (dc iw) -> p dc iw", iw=P),
                              in_=uw_d[ic]),
                          sU[sl2], waits=uw_w)

            def gate_mm(t, ic=ic, sl2=sl2):
                last = None
                dst = pair((ic % 2) * 2)
                for dc in range(DC):
                    lhsT = gslot[sl2][:, dc * 128:(dc + 1) * 128]
                    for half in range(2):
                        last = t.matmul(
                            dst[:, half * 512:(half + 1) * 512], lhsT,
                            ln2_fm[:, dc * 1024 + half * 512:
                                   dc * 1024 + (half + 1) * 512],
                            start=(dc == 0), stop=(dc == DC - 1),
                            skip_group_check=True)
                return last
            gmw = [t_g[ic], (DV, dv_ln2_done), (AC, ac_silu.get(ic - 2, 0))]
            if icg < 2 and g_i > 0:
                gmw.append((DV, dv_out[(g_i - 1, OS - 1, NT - 1)]))
            pe_gate[ic] = st("tensor", gate_mm, waits=gmw, inc=PE)

            def up_mm(t, ic=ic, sl2=sl2):
                last = None
                dst = pair(4 + (ic % 2) * 2)
                for dc in range(DC):
                    lhsT = uslot[sl2][:, dc * 128:(dc + 1) * 128]
                    for half in range(2):
                        last = t.matmul(
                            dst[:, half * 512:(half + 1) * 512], lhsT,
                            ln2_fm[:, dc * 1024 + half * 512:
                                   dc * 1024 + (half + 1) * 512],
                            start=(dc == 0), stop=(dc == DC - 1),
                            skip_group_check=True)
                return last
            umw = [t_u[ic], (DV, dv_h.get(ic - 2, 0))]
            pe_up[ic] = st("tensor", up_mm, waits=umw, inc=PE)

            ac_silu[ic] = st("scalar",
                             lambda sc, ic=ic: sc.activation(
                                 out=sg_sb[:], in_=pair((ic % 2) * 2),
                                 func=AF.Silu, scale=gs128[:]),
                             waits=((PE, pe_gate[ic]), (DV, dv_h.get(ic - 1, 0)),
                                    (DV, dv_scales)),
                             inc=AC)
            dv_h[ic] = st("vector",
                          lambda v, ic=ic, icg=icg: v.scalar_tensor_tensor(
                              out=h_fm[:, icg * 1024:(icg + 1) * 1024],
                              in0=pair(4 + (ic % 2) * 2), scalar=us128[:],
                              in1=sg_sb[:], op0=ALU.mult, op1=ALU.mult),
                          waits=((AC, ac_silu[ic]), (PE, pe_up[ic])), inc=DV)

        src_part = [x1d, pa_d, pb_d, pa_d, pb_d][g_i]
        dst_part = [pa_d, pb_d, pa_d, pb_d, out_d][g_i]
        for os_ in range(OS):
            gp = g_i * OS + os_ - 2
            dwar = pe_dmm.get((gp // OS, gp % OS, NT - 1), 0) if gp >= 0 else 0
            if g_i == 0 and os_ < 2:
                dww = [(PE, pe_attn_free), (DV, dv_ln2_done),
                       (AC, ac_e2_last)]
            else:
                dww = [(PE, dwar)]
            t_dw = dma(
                "gpsimd",
                lambda g, os_=os_, g0=gbase, gsz=gsz: g.dma_start(
                    out=dwslot[os_ % 2][:, 0:gsz * 512].rearrange(
                        "p (ic ow) -> p ic ow", ow=512),
                    in_=dw_d[os_, :, g0:g0 + gsz, :]),
                sDW[os_ % 2], waits=dww)
            for tc in range(NT):
                par = ((g_i * OS + os_) * NT + tc) % 2
                p2 = fprev(g_i, os_, tc, 2)
                w = [t_dw, (DV, dv_out.get(p2, 0))]
                if g_i == 0 and os_ == 0 and tc < 2:
                    w.append((AC, ac_silu[ic_last]))

                def d_mm(t, os_=os_, tc=tc, par=par, gsz=gsz):
                    last = None
                    for icg in range(gsz):
                        last = t.matmul(
                            bank(par),
                            h_fm[:, icg * 1024 + tc * 128:
                                 icg * 1024 + (tc + 1) * 128],
                            dwslot[os_ % 2][:, icg * 512:(icg + 1) * 512],
                            start=(icg == 0), stop=(icg == gsz - 1),
                            skip_group_check=True)
                    return last
                pe_dmm[(g_i, os_, tc)] = st("tensor", d_mm, waits=w, inc=PE)

                pw = [(DV, dv_out.get(p2, 0))]
                if g_i > 0:
                    pw.append(t_outw[(g_i - 1, os_, tc)])
                if g_i == 0 and os_ == 0 and tc < 2:
                    pw.append((DV, dv_ln2_done))
                t_prev[(g_i, os_, tc)] = dma(
                    "sync",
                    lambda s, os_=os_, tc=tc, par=par, src=src_part: s.dma_start(
                        out=prev_sb[par],
                        in_=src[tc * 128:(tc + 1) * 128,
                                os_ * 512:(os_ + 1) * 512]),
                    sPREV[par], waits=pw)
                dv_out[(g_i, os_, tc)] = st(
                    "vector",
                    lambda v, par=par: v.scalar_tensor_tensor(
                        out=outst[par], in0=bank(par), scalar=ds128[:],
                        in1=prev_sb[par], op0=ALU.mult, op1=ALU.add),
                    waits=((PE, pe_dmm[(g_i, os_, tc)]),
                           t_prev[(g_i, os_, tc)],
                           t_outw.get(p2, (CC, 0))), inc=DV)
                t_outw[(g_i, os_, tc)] = dma(
                    "sync",
                    lambda s, os_=os_, tc=tc, par=par, dst=dst_part: s.dma_start(
                        out=dst[tc * 128:(tc + 1) * 128,
                                os_ * 512:(os_ + 1) * 512],
                        in_=outst[par]),
                    sOUT[par], waits=((DV, dv_out[(g_i, os_, tc)]),))
        gbase += gsz

    st("sync", lambda s: None,
       waits=((sOUT[0], sOUT[0].n), (sOUT[1], sOUT[1].n)))
    mark("F")

    if stop is not None:
        cut = marks[stop]
        for e in ENGS:
            steps[e] = steps[e][:cut[e]]
    with nc.Block() as block:
        def runner(prog):
            def run(eng):
                for (waits, fn, sem_h, inck) in prog:
                    for (s, v) in waits:
                        eng.wait_ge(s.h, v)
                    inst = fn(eng)
                    if sem_h is not None:
                        assert inst is not None
                        inst.then_inc(sem_h, inck)
            return run
        block.sync(runner(steps["sync"]))
        block.tensor(runner(steps["tensor"]))
        block.vector(runner(steps["vector"]))
        block.scalar(runner(steps["scalar"]))
        block.gpsimd(runner(steps["gpsimd"]))

    ctx.close()
    return nc


_NC_CACHE = {}


def _get_nc():
    if "nc" not in _NC_CACHE:
        _NC_CACHE["nc"] = build_program()
    return _NC_CACHE["nc"]


def kernel(L0_attn_x, L0_attn_w_q, L0_attn_w_k, L0_attn_w_v, L0_attn_w_o,
           L0_ffn_gate_w, L0_ffn_up_w, L0_ffn_down_w,
           L0_ffn_gate_scale, L0_ffn_up_scale, L0_ffn_down_scale):
    x = np.ascontiguousarray(np.asarray(L0_attn_x, dtype=np.float32)).reshape(8192, D)

    def stage_lhs(w, nchunk):
        w = np.asarray(w, dtype=np.float32)
        return np.ascontiguousarray(
            w.reshape(nchunk, P, DC, P).transpose(0, 3, 2, 1))

    def stage_rhs(w, nchunk):
        w = np.asarray(w, dtype=np.float32)
        return np.ascontiguousarray(
            w.reshape(OS, 512, nchunk, P).transpose(0, 3, 2, 1))

    wq = stage_lhs(L0_attn_w_q, NH)
    wk = stage_lhs(L0_attn_w_k, NH)
    wv = stage_lhs(L0_attn_w_v, NH)
    wo = stage_rhs(L0_attn_w_o, NH)
    gw = stage_lhs(L0_ffn_gate_w, ICT)
    uw = stage_lhs(L0_ffn_up_w, ICT)
    dw = stage_rhs(L0_ffn_down_w, ICT)
    sc = np.array([[np.float32(L0_ffn_gate_scale),
                    np.float32(L0_ffn_up_scale),
                    np.float32(L0_ffn_down_scale)]], dtype=np.float32)

    nc = _get_nc()
    in_maps = []
    for c in range(8):
        in_maps.append({
            "x": np.ascontiguousarray(x[c * TC:(c + 1) * TC]),
            "wq": wq, "wk": wk, "wv": wv, "wo": wo,
            "gw": gw, "uw": uw, "dw": dw, "sc": sc,
        })
    res = run_bass_kernel_spmd(nc, in_maps, core_ids=list(range(8)))
    out = np.concatenate([res.results[c]["out"] for c in range(8)], axis=0)
    return out.reshape(2, 4096, D)


# version 07
# speedup vs baseline: 1.0238x; 1.0238x over previous
import contextlib
import math

import numpy as np

import concourse.bass as bass
import concourse.mybir as mybir
from concourse.bass_utils import run_bass_kernel_spmd

F32 = mybir.dt.float32
BF16 = mybir.dt.bfloat16
AF = mybir.ActivationFunctionType
ALU = mybir.AluOpType
AX = mybir.AxisListType

P = 128
D = 4096
TC = 1024
NT = TC // P
DC = D // P
NH = 32
DK = 128
KC = 32
INTER = 11008
ICT = INTER // P
OS = 8
GROUPS = [18, 17, 17, 17, 17]
INV_SQRT_DK = 1.0 / math.sqrt(DK)
EPS = 1e-5

ENGS = ("sync", "tensor", "vector", "scalar", "gpsimd")


class Sem:

    def __init__(self, handle):
        self.h = handle
        self.n = 0

    def bump(self, k):
        self.n += k
        return self.n


def build_program(stop=None, dump=None):
    nc = bass.Bass("TRN2", target_bir_lowering=False)

    x_d = nc.dram_tensor("x", [TC, D], F32, kind="ExternalInput").ap()
    wq_d = nc.dram_tensor("wq", [NH, P, DC, P], F32, kind="ExternalInput").ap()
    wk_d = nc.dram_tensor("wk", [NH, P, DC, P], F32, kind="ExternalInput").ap()
    wv_d = nc.dram_tensor("wv", [NH, P, DC, P], F32, kind="ExternalInput").ap()
    wo_d = nc.dram_tensor("wo", [OS, P, NH, 512], F32, kind="ExternalInput").ap()
    gw_d = nc.dram_tensor("gw", [ICT, P, DC, P], F32, kind="ExternalInput").ap()
    uw_d = nc.dram_tensor("uw", [ICT, P, DC, P], F32, kind="ExternalInput").ap()
    dw_d = nc.dram_tensor("dw", [OS, P, ICT, 512], F32, kind="ExternalInput").ap()
    sc_d = nc.dram_tensor("sc", [1, 3], F32, kind="ExternalInput").ap()
    out_d = nc.dram_tensor("out", [TC, D], F32, kind="ExternalOutput").ap()
    dbg_d = nc.dram_tensor("dbg", [P, 16384], F32, kind="ExternalOutput").ap() \
        if dump else None
    dbg2_d = nc.dram_tensor("dbg2", [P, 16384], F32, kind="ExternalOutput").ap() \
        if dump else None

    k_own = nc.dram_tensor("k_own", [D, TC], BF16, kind="Internal").ap()
    v_own = nc.dram_tensor("v_own", [TC, D], BF16, kind="Internal").ap()
    k_all = nc.dram_tensor("k_all", [4 * D, TC], BF16, kind="Internal").ap()
    v_all = nc.dram_tensor("v_all", [4 * TC, D], BF16, kind="Internal").ap()
    k2o_d = nc.dram_tensor("k2o", [1, NH], F32, kind="Internal").ap()
    k2a_d = nc.dram_tensor("k2a", [4, NH], F32, kind="Internal").ap()
    x1d = nc.dram_tensor("x1d", [TC, D], F32, kind="Internal").ap()
    pa_d = nc.dram_tensor("pa", [TC, D], F32, kind="Internal").ap()
    pb_d = nc.dram_tensor("pb", [TC, D], F32, kind="Internal").ap()

    ctx = contextlib.ExitStack()

    ar1 = ctx.enter_context(nc.sbuf_tensor("ar1", [P, 16384], F32))
    ar2 = ctx.enter_context(nc.sbuf_tensor("ar2", [P, 16384], F32))
    ar3 = ctx.enter_context(nc.sbuf_tensor("ar3", [P, 11264], F32))
    ar4 = ctx.enter_context(nc.sbuf_tensor("ar4", [P, 8192], F32))

    ar1_16 = ar1[:].bitcast(BF16)
    ar2_16 = ar2[:].bitcast(BF16)
    ar3_16 = ar3[:].bitcast(BF16)
    ar4_16 = ar4[:].bitcast(BF16)

    lnx_fm = ar1_16
    qsq_v = ar1_16
    attn_fm = ar1_16
    ln2_fm = ar1_16
    ar1_3d = ar1_16.rearrange("p (k t) -> p k t", t=1024)

    q_fm = ar2_16
    wo_slot = [ar2_16[:, j * 8192:(j + 1) * 8192] for j in range(3)]
    h_fm = ar2_16

    xt = [ar3[:, i * 4096:(i + 1) * 4096] for i in range(2)]
    wslot = [ar3_16[:, j * 4096:(j + 1) * 4096] for j in range(3)]
    kst = [ar3_16[:, 12288 + i * 1024: 12288 + (i + 1) * 1024] for i in range(2)]
    vfm_sb = ar3_16[:, 14336:15360]
    vtm_sb = ar3_16[:, 15360:16384]
    ktile = [ar3_16[:, i * 4096:(i + 1) * 4096] for i in range(2)]
    vtile = [ar3_16[:, 8192 + i * 4096: 8192 + (i + 1) * 4096] for i in range(2)]
    pch = [ar3_16[:, 16384 + i * 1024: 16384 + (i + 1) * 1024] for i in range(4)]
    bc_sb = ar3[:, 10240:11264]
    xs = [ar3[:, i * 512:(i + 1) * 512] for i in range(2)]
    x1s = [ar3[:, 1024 + i * 512: 1024 + (i + 1) * 512] for i in range(2)]
    x1t = [ar3[:, i * 4096:(i + 1) * 4096] for i in range(2)]
    dwslot = [ar3_16[:, i * 9216:(i + 1) * 9216] for i in range(2)]
    prev_sb = [ar3[:, 9216 + i * 512: 9216 + (i + 1) * 512] for i in range(2)]
    outst = [ar3[:, 10240 + i * 512: 10240 + (i + 1) * 512] for i in range(2)]

    scr16 = ar4_16[:, 0:4096]
    lntm = [ar4_16[:, 4096 + i * 4096: 4096 + (i + 1) * 4096] for i in range(2)]
    qrow = ar4[0:1, 2048:3072]
    ksq = ar4_16[:, 0:1024]
    zacc = ar4[:, 0:1024]
    zaccB = ar4[:, 4096:5120]
    zacc16 = ar4_16[:, 2048:3072]
    attnU = [ar4_16[:, 3072 + i * 1024: 3072 + (i + 1) * 1024] for i in range(2)]
    srowL = ar4[0:1, 6144:7168]
    srowR = ar4[0:1, 7168:8192]
    gslot = [ar4_16[:, i * 4096:(i + 1) * 4096] for i in range(2)]
    uslot = [ar4_16[:, 8192 + i * 4096: 8192 + (i + 1) * 4096] for i in range(2)]

    ident = ctx.enter_context(nc.sbuf_tensor("ident", [P, P], BF16))
    ones_col = ctx.enter_context(nc.sbuf_tensor("ones_col", [P, 1], BF16))
    ones_row = ctx.enter_context(nc.sbuf_tensor("ones_row", [1, P], F32))
    sumx = ctx.enter_context(nc.sbuf_tensor("sumx", [P, 1], F32))
    ssq = ctx.enter_context(nc.sbuf_tensor("ssq", [P, 1], F32))
    m128 = ctx.enter_context(nc.sbuf_tensor("m128", [P, 1], F32))
    std128 = ctx.enter_context(nc.sbuf_tensor("std128", [P, 1], F32))
    rstd128 = ctx.enter_context(nc.sbuf_tensor("rstd128", [P, 1], F32))
    gs128 = ctx.enter_context(nc.sbuf_tensor("gs128", [P, 1], F32))
    us128 = ctx.enter_context(nc.sbuf_tensor("us128", [P, 1], F32))
    ds128 = ctx.enter_context(nc.sbuf_tensor("ds128", [P, 1], F32))
    sc_sb = ctx.enter_context(nc.sbuf_tensor("sc_sb", [1, 3], F32))
    qn2max = ctx.enter_context(nc.sbuf_tensor("qn2max", [1, NH], F32))
    k2row = ar4[0:1, 6144:6272]
    k2all = ar4[0:1, 6272:6304]
    prodv = ar4[0:1, 6304:6336]
    biasv = ar4[0:1, 6336:6368]
    k2own = ar4[0:1, 6368:6400]
    bias_all = ar4[:, 2560:2592]
    sg_sb = ctx.enter_context(nc.sbuf_tensor("sg_sb", [P, 1024], BF16))
    eps128 = ctx.enter_context(nc.sbuf_tensor("eps128", [P, 1], F32))

    psh = ctx.enter_context(nc.psum_tensor("ps", [P, 4096], F32))
    psf = psh[:]
    ps16 = psf.bitcast(BF16)

    def bank(k):
        return psf[:, k * 512:(k + 1) * 512]

    def pair(k):
        return psf[:, k * 512:(k + 2) * 512]

    tr_p = [ps16[:, i * 1024:(i + 1) * 1024] for i in range(2)]
    ps_qn = psf[0:1, 2560:3584]
    ps_kn = [psf[0:1, 2048 + i * 512: 2048 + (i + 1) * 512] for i in range(2)]
    ps_sums = psf[0:1, 3072:4096]
    ps_bias = psf[:, 3584:3584 + NH]
    ps_scb = [psf[:, 3584 + j: 3585 + j] for j in range(3)]
    ps_vt = ps16[:, 4096:5120]

    all_sems = []

    def mksem(name):
        s = Sem(ctx.enter_context(nc.semaphore(name)))
        all_sems.append(s)
        return s

    def mkstream(name, n):
        return [mksem(f"{name}{i}") for i in range(n)]

    PE = mksem("PE")
    DV = mksem("DV")
    AC = mksem("AC")
    CC = mksem("CC")
    PL = mksem("PL")
    sSC = mksem("sSC")
    sX = mkstream("sX", 2)
    sW = mkstream("sW", 3)
    sKST = mkstream("sKST", 2)
    sVOWN = mksem("sVOWN")
    sK2 = mksem("sK2")
    sKL = mkstream("sKL", 2)
    sVL = mkstream("sVL", 2)
    sWO = mkstream("sWO", 3)
    sXS = mkstream("sXS", 2)
    sX1W = mkstream("sX1W", 2)
    sX1R = mkstream("sX1R", 2)
    sG = mkstream("sG", 2)
    sU = mkstream("sU", 2)
    sDW = mkstream("sDW", 2)
    sPREV = mkstream("sPREV", 2)
    sOUT = mkstream("sOUT", 2)
    sDBG = mksem("sDBG")

    steps = {e: [] for e in ENGS}
    waited = {e: {} for e in ENGS}

    def st(eng, fn, waits=(), inc=None, inck=1):
        w = []
        for (s, v) in waits:
            if not v or v <= 0:
                continue
            if waited[eng].get(s, 0) >= v:
                continue
            waited[eng][s] = v
            w.append((s, v))
        w = tuple(w)
        val = inc.bump(inck) if inc is not None else None
        steps[eng].append((w, fn, (inc.h if inc else None), inck))
        return val

    def dma(eng, fn, sem, waits=()):
        v = st(eng, fn, waits=waits, inc=sem, inck=16)
        return (sem, v)

    marks = {}

    def mark(name):
        marks[name] = {e: len(steps[e]) for e in ENGS}

    def maybe_dump(name, src_ap, second=None):
        if dump != name:
            return
        waits = [(S, S.n) for S in all_sems]

        def f(s, src_ap=src_ap, second=second):
            for i in range(8):
                s.dma_start(out=dbg_d[:, i * 2048:(i + 1) * 2048],
                            in_=src_ap[:, i * 2048:(i + 1) * 2048]
                            ).then_inc(sDBG.h, 16)
            if second is not None:
                for i in range(8):
                    s.dma_start(out=dbg2_d[:, i * 2048:(i + 1) * 2048],
                                in_=second[:, i * 2048:(i + 1) * 2048]
                                ).then_inc(sDBG.h, 16)
            return None
        st("sync", f, waits=waits)
        sDBG.bump(256 if second is not None else 128)

    pl_id0 = st("gpsimd", lambda g: g.memset(ident[:], 0.0), inc=PL)
    st("gpsimd",
       lambda g: g.affine_select(out=ident[:], in_=ident[:],
                                 compare_op=ALU.not_equal, fill=1.0, base=0,
                                 pattern=[[-1, P]], channel_multiplier=1),
       waits=((PL, pl_id0),), inc=PL)

    def setup_pool(g):
        g.memset(ones_col[:], 1.0)
        g.memset(eps128[:], EPS)
        return g.memset(ones_row[:], 1.0)
    pl_setup = st("gpsimd", setup_pool, inc=PL)

    t_sc = dma("sync", lambda s: s.dma_start(out=sc_sb[:], in_=sc_d[:]), sSC)

    def setup_bcast(t):
        last = None
        for j in range(3):
            last = t.matmul(ps_scb[j], ones_row[:], sc_sb[0:1, j:j + 1],
                            start=True, stop=True, skip_group_check=True)
        return last
    pe_scb = st("tensor", setup_bcast,
                waits=((PL, pl_setup), t_sc), inc=PE)

    def setup_copy(v):
        v.tensor_copy(out=gs128[:], in_=ps_scb[0])
        v.tensor_copy(out=us128[:], in_=ps_scb[1])
        return v.tensor_copy(out=ds128[:], in_=ps_scb[2])
    dv_scales = st("vector", setup_copy, waits=((PE, pe_scb),), inc=DV)

    mark("setup")

    L = {"pe_tr": {}, "dv_evac": {}, "dv_apply": {}, "ac_sq": {}}

    def emit_ln_tile(tag, i, src_tile, act_waits=()):
        sl = i % 2
        ac1 = st("scalar",
                 lambda sc, src=src_tile: sc.activation(
                     out=scr16, in_=src, func=AF.Copy, accum_out=sumx[:]),
                 waits=act_waits, inc=AC)
        dv1 = st("vector",
                 lambda v: v.tensor_scalar(out=m128[:], in0=sumx[:],
                                           scalar1=-1.0 / D, scalar2=None,
                                           op0=ALU.mult),
                 waits=((AC, ac1),), inc=DV)
        ac2a = st("scalar",
                  lambda sc, src=src_tile: sc.activation(
                      out=scr16, in_=src, func=AF.Square,
                      bias=m128[:], scale=1.0, accum_out=ssq[:]),
                  waits=((DV, dv1),), inc=AC)
        ac2 = st("scalar",
                 lambda sc: sc.activation(out=std128[:], in_=ssq[:],
                                          func=AF.Sqrt, scale=1.0 / D,
                                          bias=eps128[:]),
                 waits=((AC, ac2a),), inc=AC)
        L["ac_sq"][(tag, i)] = ac2
        dv2 = st("vector", lambda v: v.reciprocal(out=rstd128[:], in_=std128[:]),
                 waits=((AC, ac2),), inc=DV)
        war = L["pe_tr"].get((tag, i - 2, 3), 0)
        dv3 = st("vector",
                 lambda v, src=src_tile, sl=sl: v.tensor_scalar(
                     out=lntm[sl], in0=src, scalar1=m128[:], scalar2=rstd128[:],
                     op0=ALU.add, op1=ALU.mult),
                 waits=((DV, dv2), (PE, war)), inc=DV)
        L["dv_apply"][(tag, i)] = dv3
        return dv3

    def emit_ln_transposes(tag, i):
        sl = i % 2
        dv3 = L["dv_apply"][(tag, i)]
        for b in range(4):
            par = b % 2
            war = max(L["dv_evac"].get((tag, i, b - 2), 0),
                      L["dv_evac"].get((tag, i - 1, b + 2), 0) if b < 2 else 0)

            def tr(t, b=b, par=par, sl=sl):
                last = None
                for j in range(8):
                    k = b * 8 + j
                    last = t.transpose(tr_p[par][:, j * 128:(j + 1) * 128],
                                       lntm[sl][:, k * 128:(k + 1) * 128],
                                       ident[:])
                return last
            pe1 = st("tensor", tr,
                     waits=((DV, dv3), (DV, war), (PL, pl_setup)), inc=PE)
            L["pe_tr"][(tag, i, b)] = pe1

            def evac(v, b=b, par=par, i=i):
                src = tr_p[par].rearrange("p (j q) -> p j q", q=128)
                dst = ar1_3d[:, b * 8:(b + 1) * 8, i * 128:(i + 1) * 128]
                return v.tensor_copy(out=dst, in_=src)
            L["dv_evac"][(tag, i, b)] = st("vector", evac,
                                           waits=((PE, pe1),), inc=DV)
        return L["dv_evac"][(tag, i, 3)]

    t_x = {}
    for i in range(NT):
        sl = i % 2
        war = L["dv_apply"].get(("A", i - 2), 0)
        t_x[i] = dma("sync",
                     lambda s, i=i, sl=sl: s.dma_start(
                         out=xt[sl], in_=x_d[i * P:(i + 1) * P, :]),
                     sX[sl], waits=((DV, war),))
        emit_ln_tile("A", i, xt[sl], act_waits=(t_x[i],))
        emit_ln_transposes("A", i)
    dv_lnx_done = L["dv_evac"][("A", NT - 1, 3)]
    maybe_dump("A", ar1[:])
    mark("A")

    t_w = {}
    pe_mm_done = {}
    dv_kst = {}
    t_kown = {}
    dv_vfm = {}
    pe_vt = {}
    dv_vtm = {}
    t_vown = {}
    dv_q = {}

    def w_dma(src, oc, slot):
        def f(g, src=src, oc=oc, slot=slot):
            return g.dma_start(
                out=wslot[slot].rearrange("p (dc ow) -> p dc ow", ow=P),
                in_=src[oc])
        return f

    def proj_matmuls(t, slot, dst_pair):
        last = None
        for dc in range(DC):
            lhsT = wslot[slot][:, dc * 128:(dc + 1) * 128]
            for half in range(2):
                last = t.matmul(
                    dst_pair[:, half * 512:(half + 1) * 512],
                    lhsT,
                    lnx_fm[:, dc * 1024 + half * 512: dc * 1024 + (half + 1) * 512],
                    start=(dc == 0), stop=(dc == DC - 1),
                    skip_group_check=True)
        return last

    ac_ksq = {}
    kn_pe = {}
    dv_k2 = {}

    def emit_kn(oc):
        def kn_mm(t):
            last = None
            for j in range(2):
                last = t.matmul(ps_kn[j], ones_col[:],
                                ksq[:, j * 512:(j + 1) * 512],
                                start=True, stop=True, skip_group_check=True)
            return last
        kn_pe[oc] = st("tensor", kn_mm,
                       waits=((AC, ac_ksq[oc]), (DV, dv_k2.get(oc - 1, 0))),
                       inc=PE)
        dv_k2[oc] = st("vector",
                       lambda v, oc=oc: v.tensor_reduce(
                           out=k2own[0:1, oc:oc + 1], in_=psf[0:1, 2048:3072],
                           axis=AX.X, op=ALU.max),
                       waits=((PE, kn_pe[oc]),), inc=DV)

    for oc in range(NH):
        widx = oc
        slot = widx % 3
        kw = [(PE, pe_mm_done.get(widx - 3, 0))]
        if widx < 3:
            kw.append((DV, L["dv_apply"][("A", NT - 1)]))
            kw.append((AC, L["ac_sq"][("A", NT - 1)]))
        t_w[widx] = dma("gpsimd", w_dma(wk_d, oc, slot), sW[slot], waits=kw)
        pe_mm_done[widx] = st(
            "tensor",
            lambda t, slot=slot, oc=oc: proj_matmuls(t, slot, pair((oc % 2) * 2)),
            waits=(t_w[widx], (DV, dv_lnx_done), (DV, dv_kst.get(oc - 2, 0))),
            inc=PE)
        if oc >= 1:
            emit_kn(oc - 1)
        dv_kst[oc] = st(
            "vector",
            lambda v, oc=oc: v.tensor_copy(out=kst[oc % 2], in_=pair((oc % 2) * 2)),
            waits=((PE, pe_mm_done[widx]),
                   t_kown.get(oc - 2, (CC, 0)),
                   (AC, ac_ksq.get(oc - 2, 0))), inc=DV)
        t_kown[oc] = dma(
            "sync",
            lambda s, oc=oc: s.dma_start(
                out=k_own[oc * 128:(oc + 1) * 128, :], in_=kst[oc % 2]),
            sKST[oc % 2], waits=((DV, dv_kst[oc]),))
        ac_ksq[oc] = st("scalar",
                        lambda sc, oc=oc: sc.activation(
                            out=ksq, in_=kst[oc % 2], func=AF.Square),
                        waits=((DV, dv_kst[oc]), (PE, kn_pe.get(oc - 1, 0))),
                        inc=AC)
    emit_kn(NH - 1)
    t_k2w = dma("sync",
                lambda s: s.dma_start(out=k2o_d[:], in_=k2own[0:1, :]),
                sK2, waits=((DV, dv_k2[NH - 1]),))

    rg = [[0, 1, 2, 3], [4, 5, 6, 7]]
    cc_k2 = cc_k = None
    for oc in range(NH):
        widx = NH + oc
        slot = widx % 3
        t_w[widx] = dma("gpsimd", w_dma(wv_d, oc, slot), sW[slot],
                        waits=((PE, pe_mm_done.get(widx - 3, 0)),))
        if oc == 16:
            cc_k2 = st("gpsimd",
                       lambda g: g.collective_compute(
                           "AllGather", ALU.bypass, replica_groups=rg,
                           ins=[k2o_d.opt()], outs=[k2a_d.opt()]),
                       waits=(t_k2w,), inc=CC)
            cc_k = st("gpsimd",
                      lambda g: g.collective_compute(
                          "AllGather", ALU.bypass, replica_groups=rg,
                          ins=[k_own.opt()], outs=[k_all.opt()]),
                      waits=((sKST[0], sKST[0].n), (sKST[1], sKST[1].n)),
                      inc=CC)
        mm_w = [t_w[widx], (DV, dv_vfm.get(oc - 2, 0))]
        if oc < 2:
            mm_w.append((DV, dv_kst[NH - 1]))
        pe_mm_done[widx] = st(
            "tensor",
            lambda t, slot=slot, oc=oc: proj_matmuls(t, slot, pair((oc % 2) * 2)),
            waits=mm_w, inc=PE)
        dv_vfm[oc] = st(
            "vector",
            lambda v, oc=oc: v.tensor_copy(out=vfm_sb, in_=pair((oc % 2) * 2)),
            waits=((PE, pe_mm_done[widx]), (PE, pe_vt.get(oc - 1, 0))), inc=DV)

        def vtr(t, oc=oc):
            last = None
            for j in range(8):
                last = t.transpose(ps_vt[:, j * 128:(j + 1) * 128],
                                   vfm_sb[:, j * 128:(j + 1) * 128], ident[:])
            return last
        pe_vt[oc] = st("tensor", vtr,
                       waits=((DV, dv_vfm[oc]), (DV, dv_vtm.get(oc - 1, 0))),
                       inc=PE)
        dv_vtm[oc] = st(
            "vector",
            lambda v: v.tensor_copy(out=vtm_sb, in_=ps_vt),
            waits=((PE, pe_vt[oc]), t_vown.get(oc - 1, (CC, 0))), inc=DV)
        t_vown[oc] = dma(
            "sync",
            lambda s, oc=oc: s.dma_start(
                out=v_own[:, oc * 128:(oc + 1) * 128].rearrange(
                    "(b p) o -> p b o", p=P),
                in_=vtm_sb.rearrange("p (b o) -> p b o", o=P)),
            sVOWN, waits=((DV, dv_vtm[oc]),))

    cc_v = st("gpsimd",
              lambda g: g.collective_compute(
                  "AllGather", ALU.bypass, replica_groups=rg,
                  ins=[v_own.opt()], outs=[v_all.opt()]),
              waits=((sVOWN, sVOWN.n),), inc=CC)

    for oc in range(NH):
        widx = 2 * NH + oc
        slot = widx % 3
        t_w[widx] = dma("gpsimd", w_dma(wq_d, oc, slot), sW[slot],
                        waits=((PE, pe_mm_done.get(widx - 3, 0)),))
        mm_w = [t_w[widx], (DV, dv_q.get(oc - 2, 0))]
        if oc < 2:
            mm_w.append((DV, dv_vfm[NH - 1]))
        pe_mm_done[widx] = st(
            "tensor",
            lambda t, slot=slot, oc=oc: proj_matmuls(t, slot, pair((oc % 2) * 2)),
            waits=mm_w, inc=PE)
        dv_q[oc] = st(
            "vector",
            lambda v, oc=oc: v.tensor_copy(
                out=q_fm[:, oc * 1024:(oc + 1) * 1024], in_=pair((oc % 2) * 2)),
            waits=((PE, pe_mm_done[widx]),), inc=DV)
    pe_q_done = pe_mm_done[3 * NH - 1]

    ac_qsq = st("scalar",
                lambda sc: sc.activation(out=qsq_v, in_=q_fm, func=AF.Square),
                waits=((DV, dv_q[NH - 1]),), inc=AC)
    pe_qn = {}
    dv_qrow = {}
    dv_qn = {}
    for h in range(NH):
        def qn_mm(t, h=h):
            last = None
            for half in range(2):
                last = t.matmul(
                    ps_qn[:, half * 512:(half + 1) * 512], ones_col[:],
                    qsq_v[:, h * 1024 + half * 512: h * 1024 + (half + 1) * 512],
                    start=True, stop=True, skip_group_check=True)
            return last
        pe_qn[h] = st("tensor", qn_mm,
                      waits=((AC, ac_qsq), (DV, dv_qrow.get(h - 1, 0))), inc=PE)
        dv_qrow[h] = st("vector",
                        lambda v: v.tensor_copy(out=qrow, in_=ps_qn),
                        waits=((PE, pe_qn[h]), (DV, dv_qn.get(h - 1, 0))), inc=DV)
        dv_qn[h] = st("vector",
                      lambda v, h=h: v.tensor_reduce(
                          out=qn2max[0:1, h:h + 1], in_=qrow, axis=AX.X,
                          op=ALU.max),
                      waits=((DV, dv_qrow[h]),), inc=DV)

    t_k2r = dma("sync",
                lambda s: s.dma_start(out=k2row,
                                      in_=k2a_d.rearrange("c h -> (c h)")),
                sK2, waits=((CC, cc_k2),))
    dv_k2c = st("vector",
                lambda v: v.tensor_tensor(out=k2all, in0=k2row[0:1, 0:NH],
                                          in1=k2row[0:1, NH:2 * NH],
                                          op=ALU.max),
                waits=(t_k2r,), inc=DV)
    dv_k2c = st("vector",
                lambda v: v.tensor_tensor(out=k2all, in0=k2all,
                                          in1=k2row[0:1, 2 * NH:3 * NH],
                                          op=ALU.max),
                waits=((DV, dv_k2c),), inc=DV)
    dv_k2c = st("vector",
                lambda v: v.tensor_tensor(out=k2all, in0=k2all,
                                          in1=k2row[0:1, 3 * NH:4 * NH],
                                          op=ALU.max),
                waits=((DV, dv_k2c),), inc=DV)
    dv_prod = st("vector",
                 lambda v: v.tensor_tensor(out=prodv, in0=qn2max[:],
                                           in1=k2all, op=ALU.mult),
                 waits=((DV, dv_k2c), (DV, dv_qn[NH - 1])), inc=DV)
    ac_biasv = st("scalar",
                  lambda sc: sc.activation(out=biasv, in_=prodv,
                                           func=AF.Sqrt, scale=1.0 / DK),
                  waits=((DV, dv_prod),), inc=AC)
    pe_bcb = st("tensor",
                lambda t: t.matmul(ps_bias, ones_row[:], biasv,
                                   start=True, stop=True,
                                   skip_group_check=True),
                waits=((AC, ac_biasv), (DV, dv_scales)), inc=PE)
    dv_biasall = st("vector",
                    lambda v: v.tensor_scalar(out=bias_all, in0=ps_bias,
                                              scalar1=-1.0, scalar2=None,
                                              op0=ALU.mult),
                    waits=((PE, pe_bcb),), inc=DV)
    maybe_dump("B", ar2[:], second=ar1[:])
    mark("B")
    mark("Q")

    t_k = {}
    t_v = {}
    dv_attn = {}
    dv_bc = {}
    pe_st_last = {}
    pe_pv_last = {}
    pe_sums_last = {}
    pe_bc = {}
    dv_evacU = {}
    dv_srow = {}
    ac_rec = {}
    ac_exp_l = {}
    dvD = [0]
    plD = [PL.n]

    t_wo = {}

    def emit_wo(widx, extra_waits=()):
        os_, half = widx // 2, widx % 2
        slot = widx % 3
        t_wo[widx] = dma(
            "gpsimd",
            lambda g, os_=os_, half=half, slot=slot: g.dma_start(
                out=wo_slot[slot].rearrange("p (oc ow) -> p oc ow", ow=512),
                in_=wo_d[os_, :, half * 16:(half + 1) * 16, :]),
            sWO[slot], waits=extra_waits)

    def dvst(fn, waits=(), ):
        v = st("vector", fn, waits=tuple(waits) + ((DV, dvD[0]),), inc=DV)
        dvD[0] = v
        return v

    def emit_psums(cp, h, psums, exp_ac):
        w = [(AC, exp_ac[cp])]
        if cp == 3:
            if h == 0:
                w += [(DV, dv_qrow[NH - 1]), (DV, dv_qn[NH - 1]),
                      (DV, dv_biasall)]
            else:
                w.append((DV, dv_bc[h - 1]))

        def sums_mm(t, cp=cp):
            last = None
            for j in range(2):
                last = t.matmul(ps_sums[:, j * 512:(j + 1) * 512], ones_col[:],
                                pch[cp % 4][:, j * 512:(j + 1) * 512],
                                start=(cp == 3), stop=False,
                                skip_group_check=True)
            return last
        psums[cp] = st("tensor", sums_mm, waits=w, inc=PE)

    def emit_merge(h):
        def merge_mm(t):
            last = None
            for j in range(2):
                last = t.matmul(ps_sums[:, j * 512:(j + 1) * 512], ones_col[:],
                                zacc16[:, j * 512:(j + 1) * 512],
                                start=False, stop=True, skip_group_check=True)
            return last
        pe_sums_last[h] = st("tensor", merge_mm,
                             waits=((DV, zcast[h]),), inc=PE)

    def emit_tail_dv(h):
        dv_bc[h] = dvst(lambda v: v.tensor_copy(out=bc_sb, in_=psf[:, 3072:4096]),
                        waits=((PE, pe_bc[h]),))
        dv_attn[h] = dvst(
            lambda v, h=h: v.tensor_tensor(
                out=attn_fm[:, h * 1024:(h + 1) * 1024],
                in0=attnU[h % 2], in1=bc_sb, op=ALU.mult))

    def emit_tail_ac(h):
        st("scalar", lambda sc: sc.activation(out=srowR, in_=srowL, func=AF.Ln),
           waits=((DV, dv_srow[h]),), inc=AC)
        ac_rec[h] = st("scalar",
                       lambda sc: sc.activation(out=srowL, in_=srowR,
                                                func=AF.Exp, scale=-1.0),
                       inc=AC)

    def emit_bc(h):
        def bc_mm(t):
            last = None
            for j in range(2):
                last = t.matmul(psf[:, 3072 + j * 512: 3072 + (j + 1) * 512],
                                ones_row[:], srowL[0:1, j * 512:(j + 1) * 512],
                                start=True, stop=True, skip_group_check=True)
            return last
        pe_bc[h] = st("tensor", bc_mm,
                      waits=((AC, ac_rec[h]), (DV, dv_srow[h])), inc=PE)

    zcast = {}
    for h in range(NH):
        sl = h % 2

        def kload(s, h=h, sl=sl):
            for j in range(4):
                s.dma_start(
                    out=ktile[sl][:, j * 1024:(j + 1) * 1024],
                    in_=k_all[j * D + h * 128: j * D + (h + 1) * 128, :]
                ).then_inc(sKL[sl].h, 16)
            return None
        klw = [(CC, cc_k), (PE, pe_st_last.get(h - 2, 0))]
        if h < 2:
            klw.append((PE, pe_q_done))
        st("sync", kload, waits=klw)
        t_k[h] = (sKL[sl], sKL[sl].bump(64))

        def vload(s, h=h, sl=sl):
            for half in range(2):
                s.dma_start(
                    out=vtile[sl][:, half * 2048:(half + 1) * 2048].rearrange(
                        "p (kc dv) -> p kc dv", dv=128),
                    in_=v_all[half * 2048:(half + 1) * 2048,
                              h * 128:(h + 1) * 128].rearrange(
                        "(kc p) dv -> p kc dv", p=P)
                ).then_inc(sVL[sl].h, 16)
            return None
        vlw = [(CC, cc_v), (PE, pe_pv_last.get(h - 2, 0))]
        if h < 2:
            vlw.append((PE, pe_q_done))
            vlw.append((DV, dv_vtm[NH - 1]))
            vlw.append((AC, ac_ksq[NH - 1]))
        st("sync", vload, waits=vlw)
        t_v[h] = (sVL[sl], sVL[sl].bump(32))

        st_pe = {}
        exp_ac = {}
        pv_pe = {}
        zadd = {}
        prev_exp = ac_exp_l.get(h - 1, {})

        def emit_st(c):
            w = [t_k[h]]
            if c >= 2:
                w.append((AC, exp_ac[c - 2]))
            elif h > 0:
                w.append((AC, prev_exp[30 + c]))
            elif c == 0:
                w.append((DV, dv_q[NH - 1]))

            def st_mm(t, c=c, sl=sl, h=h):
                last = None
                for half in range(2):
                    last = t.matmul(
                        pair((c % 2) * 2)[:, half * 512:(half + 1) * 512],
                        ktile[sl][:, c * 128:(c + 1) * 128],
                        q_fm[:, h * 1024 + half * 512: h * 1024 + (half + 1) * 512],
                        start=True, stop=True, skip_group_check=True)
                return last
            st_pe[c] = st("tensor", st_mm, waits=w, inc=PE)

        def emit_pv(c):
            w = [(AC, exp_ac[c]), t_v[h]]
            if c == 0:
                if h > 0:
                    w.append((DV, dv_evacU[h - 1]))
                else:
                    w += [(DV, dv_vtm[NH - 1]), (DV, dv_k2[NH - 1]),
                          (DV, dv_qrow[NH - 1])]

            def pv_mm(t, c=c, sl=sl):
                last = None
                for half in range(2):
                    last = t.matmul(
                        pair(4)[:, half * 512:(half + 1) * 512],
                        vtile[sl][:, c * 128:(c + 1) * 128],
                        pch[c % 4][:, half * 512:(half + 1) * 512],
                        start=(c == 0), stop=(c == KC - 1),
                        skip_group_check=True)
                return last
            pv_pe[c] = st("tensor", pv_mm, waits=w, inc=PE)

        psums = {}
        dvadd = {}
        for c in range(KC):
            if c >= 2:
                emit_pv(c - 2)
            emit_st(c)
            if c == 3 and h > 0:
                emit_merge(h - 1)
            if c == 6 and h > 0:
                emit_bc(h - 1)
            if c >= 7 and c % 2 == 1:
                emit_psums(c - 4, h, psums, exp_ac)

            if c == 4 and h > 0:
                emit_tail_ac(h - 1)
            w = [(PE, st_pe[c])]
            if c >= 4:
                prev = c - 4
                if prev >= 3 and prev % 2 == 1:
                    w.append((PE, psums[prev]))
                else:
                    w.append((DV, dvadd[prev]))
            elif h > 0:
                w.append((PE, pe_pv_last[h - 1]))
                w.append((DV, zcast[h - 1]))
            if h == 0 and c == 0:
                w.append((DV, dv_biasall))
            exp_ac[c] = st("scalar",
                           lambda sc, c=c, h=h: sc.activation(
                               out=pch[c % 4], in_=pair((c % 2) * 2),
                               func=AF.Exp, bias=bias_all[:, h:h + 1],
                               scale=INV_SQRT_DK),
                           waits=w, inc=AC)

            if c == 3 and h > 0:
                dv_srow[h - 1] = dvst(
                    lambda v: v.tensor_copy(out=srowL, in_=ps_sums),
                    waits=((PE, pe_sums_last[h - 1]),))
            if c == 6 and h > 0:
                emit_tail_dv(h - 1)
            if c % 2 == 0 or c == 1:
                if c == 0:
                    dvadd[c] = dvst(
                        lambda v, c=c: v.tensor_copy(out=zacc, in_=pch[c % 4]),
                        waits=((AC, exp_ac[c]),))
                else:
                    dvadd[c] = dvst(
                        lambda v, c=c: v.tensor_tensor(out=zacc, in0=zacc,
                                                       in1=pch[c % 4],
                                                       op=ALU.add),
                        waits=((AC, exp_ac[c]),))

        emit_pv(KC - 2)
        emit_pv(KC - 1)
        emit_psums(KC - 3, h, psums, exp_ac)
        emit_psums(KC - 1, h, psums, exp_ac)
        pe_st_last[h] = st_pe[KC - 1]
        pe_pv_last[h] = pv_pe[KC - 1]
        ac_exp_l[h] = exp_ac

        zcast[h] = dvst(
            lambda v: v.tensor_copy(out=zacc16, in_=zacc),
            waits=((PE, pe_sums_last.get(h - 1, 0)),))
        dv_evacU[h] = dvst(
            lambda v, h=h: v.tensor_copy(out=attnU[h % 2], in_=pair(4)),
            waits=((PE, pe_pv_last[h]),))
        if h in (7, 15, 23):
            emit_wo((h - 7) // 8, extra_waits=((PE, pe_st_last[h]),))

    emit_merge(NH - 1)
    dv_srow[NH - 1] = dvst(
        lambda v: v.tensor_copy(out=srowL, in_=ps_sums),
        waits=((PE, pe_sums_last[NH - 1]),))
    emit_tail_ac(NH - 1)
    emit_bc(NH - 1)
    emit_tail_dv(NH - 1)

    ac_rec_last = AC.n
    maybe_dump("D", ar1[:])
    mark("D")

    pe_o = {}
    t_xs = {}
    dv_x1 = {}
    t_x1w = {}

    peE0 = PE.n
    for widx in range(3, 2 * OS):
        war = peE0 + ((widx - 3) // 2 + 1) * NT
        emit_wo(widx, extra_waits=((PE, war),))

    def eprev(os_, tc, k):
        i = os_ * NT + tc - k
        return (i // NT, i % NT) if i >= 0 else None

    for os_ in range(OS):
        for tc in range(NT):
            par = (os_ * NT + tc) % 2
            p2 = eprev(os_, tc, 2)
            w = [t_wo[os_ * 2], t_wo[os_ * 2 + 1],
                 (DV, dv_attn[NH - 1]),
                 (DV, dv_x1.get(p2, 0))]

            def o_mm(t, os_=os_, tc=tc, par=par):
                last = None
                for oc in range(NH):
                    slot = (os_ * 2 + oc // 16) % 3
                    last = t.matmul(
                        bank(par),
                        attn_fm[:, oc * 1024 + tc * 128: oc * 1024 + (tc + 1) * 128],
                        wo_slot[slot][:, (oc % 16) * 512:((oc % 16) + 1) * 512],
                        start=(oc == 0), stop=(oc == NH - 1),
                        skip_group_check=True)
                return last
            pe_o[(os_, tc)] = st("tensor", o_mm, waits=w, inc=PE)
            assert pe_o[(os_, tc)] == peE0 + os_ * NT + tc + 1

            xw = [(DV, dv_x1.get(p2, 0))]
            if os_ * NT + tc < 2:
                xw.append((PE, pe_sums_last[NH - 1]))
                xw.append((DV, dv_attn[NH - 1]))
            t_xs[(os_, tc)] = dma(
                "sync",
                lambda s, os_=os_, tc=tc, par=par: s.dma_start(
                    out=xs[par],
                    in_=x_d[tc * 128:(tc + 1) * 128, os_ * 512:(os_ + 1) * 512]),
                sXS[par], waits=xw)
            dv_x1[(os_, tc)] = st(
                "vector",
                lambda v, par=par: v.tensor_tensor(
                    out=x1s[par], in0=bank(par), in1=xs[par], op=ALU.add),
                waits=((PE, pe_o[(os_, tc)]), t_xs[(os_, tc)],
                       t_x1w.get(p2, (CC, 0))), inc=DV)
            t_x1w[(os_, tc)] = dma(
                "sync",
                lambda s, os_=os_, tc=tc, par=par: s.dma_start(
                    out=x1d[tc * 128:(tc + 1) * 128, os_ * 512:(os_ + 1) * 512],
                    in_=x1s[par]),
                sX1W[par], waits=((DV, dv_x1[(os_, tc)]),))

    pe_attn_free = pe_o[(OS - 1, NT - 1)]
    mark("E")

    t_x1r = {}
    for i in range(NT):
        sl = i % 2
        rw = [(sX1W[0], sX1W[0].n), (sX1W[1], sX1W[1].n),
              (DV, L["dv_apply"].get(("E2", i - 2), 0))]
        t_x1r[i] = dma("sync",
                       lambda s, i=i, sl=sl: s.dma_start(
                           out=x1t[sl], in_=x1d[i * P:(i + 1) * P, :]),
                       sX1R[sl], waits=rw)
        extra = [t_x1r[i]]
        if i == 0:
            extra += [(PE, pe_attn_free), (AC, ac_rec_last)]
        emit_ln_tile("E2", i, x1t[sl], act_waits=tuple(extra))
        emit_ln_transposes("E2", i)
    dv_ln2_done = L["dv_evac"][("E2", NT - 1, 3)]
    ac_e2_last = L["ac_sq"][("E2", NT - 1)]
    maybe_dump("E2", ar1[:])
    mark("E2")

    t_g = {}
    t_u = {}
    pe_gate = {}
    pe_up = {}
    ac_silu = {}
    dv_h = {}
    pe_dmm = {}
    dv_out = {}
    t_outw = {}
    t_prev = {}

    def fprev(g_i, os_, tc, k):
        i = (g_i * OS + os_) * NT + tc - k
        if i < 0:
            return None
        t = i % NT
        i //= NT
        return (i // OS, i % OS, t)

    gbase = 0
    for g_i, gsz in enumerate(GROUPS):
        ic_last = gbase + gsz - 1
        for icg in range(gsz):
            ic = gbase + icg
            sl2 = ic % 2
            gw_w = [(PE, pe_gate.get(ic - 2, 0))]
            uw_w = [(PE, pe_up.get(ic - 2, 0))]
            if ic == 0:
                gw_w += [(DV, dv_ln2_done), (AC, ac_e2_last)]
                uw_w += [(DV, dv_ln2_done), (AC, ac_e2_last)]
            t_g[ic] = dma("gpsimd",
                          lambda g, ic=ic, sl2=sl2: g.dma_start(
                              out=gslot[sl2].rearrange("p (dc iw) -> p dc iw", iw=P),
                              in_=gw_d[ic]),
                          sG[sl2], waits=gw_w)
            t_u[ic] = dma("gpsimd",
                          lambda g, ic=ic, sl2=sl2: g.dma_start(
                              out=uslot[sl2].rearrange("p (dc iw) -> p dc iw", iw=P),
                              in_=uw_d[ic]),
                          sU[sl2], waits=uw_w)

            def gate_mm(t, ic=ic, sl2=sl2):
                last = None
                dst = pair((ic % 2) * 2)
                for dc in range(DC):
                    lhsT = gslot[sl2][:, dc * 128:(dc + 1) * 128]
                    for half in range(2):
                        last = t.matmul(
                            dst[:, half * 512:(half + 1) * 512], lhsT,
                            ln2_fm[:, dc * 1024 + half * 512:
                                   dc * 1024 + (half + 1) * 512],
                            start=(dc == 0), stop=(dc == DC - 1),
                            skip_group_check=True)
                return last
            gmw = [t_g[ic], (DV, dv_ln2_done), (AC, ac_silu.get(ic - 2, 0))]
            if icg < 2 and g_i > 0:
                gmw.append((DV, dv_out[(g_i - 1, OS - 1, NT - 1)]))
            pe_gate[ic] = st("tensor", gate_mm, waits=gmw, inc=PE)

            def up_mm(t, ic=ic, sl2=sl2):
                last = None
                dst = pair(4 + (ic % 2) * 2)
                for dc in range(DC):
                    lhsT = uslot[sl2][:, dc * 128:(dc + 1) * 128]
                    for half in range(2):
                        last = t.matmul(
                            dst[:, half * 512:(half + 1) * 512], lhsT,
                            ln2_fm[:, dc * 1024 + half * 512:
                                   dc * 1024 + (half + 1) * 512],
                            start=(dc == 0), stop=(dc == DC - 1),
                            skip_group_check=True)
                return last
            umw = [t_u[ic], (DV, dv_h.get(ic - 2, 0))]
            pe_up[ic] = st("tensor", up_mm, waits=umw, inc=PE)

            ac_silu[ic] = st("scalar",
                             lambda sc, ic=ic: sc.activation(
                                 out=sg_sb[:], in_=pair((ic % 2) * 2),
                                 func=AF.Silu, scale=gs128[:]),
                             waits=((PE, pe_gate[ic]), (DV, dv_h.get(ic - 1, 0)),
                                    (DV, dv_scales)),
                             inc=AC)
            dv_h[ic] = st("vector",
                          lambda v, ic=ic, icg=icg: v.scalar_tensor_tensor(
                              out=h_fm[:, icg * 1024:(icg + 1) * 1024],
                              in0=pair(4 + (ic % 2) * 2), scalar=us128[:],
                              in1=sg_sb[:], op0=ALU.mult, op1=ALU.mult),
                          waits=((AC, ac_silu[ic]), (PE, pe_up[ic])), inc=DV)

        src_part = [x1d, pa_d, pb_d, pa_d, pb_d][g_i]
        dst_part = [pa_d, pb_d, pa_d, pb_d, out_d][g_i]
        for os_ in range(OS):
            gp = g_i * OS + os_ - 2
            dwar = pe_dmm.get((gp // OS, gp % OS, NT - 1), 0) if gp >= 0 else 0
            if g_i == 0 and os_ < 2:
                dww = [(PE, pe_attn_free), (DV, dv_ln2_done),
                       (AC, ac_e2_last)]
            else:
                dww = [(PE, dwar)]
            t_dw = dma(
                "gpsimd",
                lambda g, os_=os_, g0=gbase, gsz=gsz: g.dma_start(
                    out=dwslot[os_ % 2][:, 0:gsz * 512].rearrange(
                        "p (ic ow) -> p ic ow", ow=512),
                    in_=dw_d[os_, :, g0:g0 + gsz, :]),
                sDW[os_ % 2], waits=dww)
            for tc in range(NT):
                par = ((g_i * OS + os_) * NT + tc) % 2
                p2 = fprev(g_i, os_, tc, 2)
                w = [t_dw, (DV, dv_out.get(p2, 0))]
                if g_i == 0 and os_ == 0 and tc < 2:
                    w.append((AC, ac_silu[ic_last]))

                def d_mm(t, os_=os_, tc=tc, par=par, gsz=gsz):
                    last = None
                    for icg in range(gsz):
                        last = t.matmul(
                            bank(par),
                            h_fm[:, icg * 1024 + tc * 128:
                                 icg * 1024 + (tc + 1) * 128],
                            dwslot[os_ % 2][:, icg * 512:(icg + 1) * 512],
                            start=(icg == 0), stop=(icg == gsz - 1),
                            skip_group_check=True)
                    return last
                pe_dmm[(g_i, os_, tc)] = st("tensor", d_mm, waits=w, inc=PE)

                pw = [(DV, dv_out.get(p2, 0))]
                if g_i > 0:
                    pw.append(t_outw[(g_i - 1, os_, tc)])
                if g_i == 0 and os_ == 0 and tc < 2:
                    pw.append((DV, dv_ln2_done))
                t_prev[(g_i, os_, tc)] = dma(
                    "sync",
                    lambda s, os_=os_, tc=tc, par=par, src=src_part: s.dma_start(
                        out=prev_sb[par],
                        in_=src[tc * 128:(tc + 1) * 128,
                                os_ * 512:(os_ + 1) * 512]),
                    sPREV[par], waits=pw)
                dv_out[(g_i, os_, tc)] = st(
                    "vector",
                    lambda v, par=par: v.scalar_tensor_tensor(
                        out=outst[par], in0=bank(par), scalar=ds128[:],
                        in1=prev_sb[par], op0=ALU.mult, op1=ALU.add),
                    waits=((PE, pe_dmm[(g_i, os_, tc)]),
                           t_prev[(g_i, os_, tc)],
                           t_outw.get(p2, (CC, 0))), inc=DV)
                t_outw[(g_i, os_, tc)] = dma(
                    "sync",
                    lambda s, os_=os_, tc=tc, par=par, dst=dst_part: s.dma_start(
                        out=dst[tc * 128:(tc + 1) * 128,
                                os_ * 512:(os_ + 1) * 512],
                        in_=outst[par]),
                    sOUT[par], waits=((DV, dv_out[(g_i, os_, tc)]),))
        gbase += gsz

    st("sync", lambda s: None,
       waits=((sOUT[0], sOUT[0].n), (sOUT[1], sOUT[1].n)))
    mark("F")

    if stop is not None:
        cut = marks[stop]
        for e in ENGS:
            steps[e] = steps[e][:cut[e]]
    with nc.Block() as block:
        def runner(prog):
            def run(eng):
                for (waits, fn, sem_h, inck) in prog:
                    for (s, v) in waits:
                        eng.wait_ge(s.h, v)
                    inst = fn(eng)
                    if sem_h is not None:
                        assert inst is not None
                        inst.then_inc(sem_h, inck)
            return run
        block.sync(runner(steps["sync"]))
        block.tensor(runner(steps["tensor"]))
        block.vector(runner(steps["vector"]))
        block.scalar(runner(steps["scalar"]))
        block.gpsimd(runner(steps["gpsimd"]))

    ctx.close()
    return nc


_NC_CACHE = {}


def _get_nc():
    if "nc" not in _NC_CACHE:
        _NC_CACHE["nc"] = build_program()
    return _NC_CACHE["nc"]


def kernel(L0_attn_x, L0_attn_w_q, L0_attn_w_k, L0_attn_w_v, L0_attn_w_o,
           L0_ffn_gate_w, L0_ffn_up_w, L0_ffn_down_w,
           L0_ffn_gate_scale, L0_ffn_up_scale, L0_ffn_down_scale):
    x = np.ascontiguousarray(np.asarray(L0_attn_x, dtype=np.float32)).reshape(8192, D)

    def stage_lhs(w, nchunk):
        w = np.asarray(w, dtype=np.float32)
        return np.ascontiguousarray(
            w.reshape(nchunk, P, DC, P).transpose(0, 3, 2, 1))

    def stage_rhs(w, nchunk):
        w = np.asarray(w, dtype=np.float32)
        return np.ascontiguousarray(
            w.reshape(OS, 512, nchunk, P).transpose(0, 3, 2, 1))

    wq = stage_lhs(L0_attn_w_q, NH)
    wk = stage_lhs(L0_attn_w_k, NH)
    wv = stage_lhs(L0_attn_w_v, NH)
    wo = stage_rhs(L0_attn_w_o, NH)
    gw = stage_lhs(L0_ffn_gate_w, ICT)
    uw = stage_lhs(L0_ffn_up_w, ICT)
    dw = stage_rhs(L0_ffn_down_w, ICT)
    sc = np.array([[np.float32(L0_ffn_gate_scale),
                    np.float32(L0_ffn_up_scale),
                    np.float32(L0_ffn_down_scale)]], dtype=np.float32)

    nc = _get_nc()
    in_maps = []
    for c in range(8):
        in_maps.append({
            "x": np.ascontiguousarray(x[c * TC:(c + 1) * TC]),
            "wq": wq, "wk": wk, "wv": wv, "wo": wo,
            "gw": gw, "uw": uw, "dw": dw, "sc": sc,
        })
    res = run_bass_kernel_spmd(nc, in_maps, core_ids=list(range(8)))
    out = np.concatenate([res.results[c]["out"] for c in range(8)], axis=0)
    return out.reshape(2, 4096, D)


# version 08
# speedup vs baseline: 1.0410x; 1.0168x over previous
import contextlib
import math

import numpy as np

import concourse.bass as bass
import concourse.mybir as mybir
from concourse.bass_utils import run_bass_kernel_spmd

F32 = mybir.dt.float32
BF16 = mybir.dt.bfloat16
AF = mybir.ActivationFunctionType
ALU = mybir.AluOpType
AX = mybir.AxisListType

P = 128
D = 4096
TC = 1024
NT = TC // P
DC = D // P
NH = 32
DK = 128
KC = 32
INTER = 11008
ICT = INTER // P
OS = 8
GROUPS = [18, 17, 17, 17, 17]
INV_SQRT_DK = 1.0 / math.sqrt(DK)
EPS = 1e-5

ENGS = ("sync", "tensor", "vector", "scalar", "gpsimd")


class Sem:

    def __init__(self, handle):
        self.h = handle
        self.n = 0

    def bump(self, k):
        self.n += k
        return self.n


def build_program(stop=None, dump=None):
    nc = bass.Bass("TRN2", target_bir_lowering=False)

    x_d = nc.dram_tensor("x", [TC, D], F32, kind="ExternalInput").ap()
    wq_d = nc.dram_tensor("wq", [NH, P, DC, P], F32, kind="ExternalInput").ap()
    wk_d = nc.dram_tensor("wk", [NH, P, DC, P], F32, kind="ExternalInput").ap()
    wv_d = nc.dram_tensor("wv", [NH, P, DC, P], F32, kind="ExternalInput").ap()
    wo_d = nc.dram_tensor("wo", [OS, P, NH, 512], F32, kind="ExternalInput").ap()
    gw_d = nc.dram_tensor("gw", [ICT, P, DC, P], F32, kind="ExternalInput").ap()
    uw_d = nc.dram_tensor("uw", [ICT, P, DC, P], F32, kind="ExternalInput").ap()
    dw_d = nc.dram_tensor("dw", [OS, P, ICT, 512], F32, kind="ExternalInput").ap()
    sc_d = nc.dram_tensor("sc", [1, 3], F32, kind="ExternalInput").ap()
    out_d = nc.dram_tensor("out", [TC, D], F32, kind="ExternalOutput").ap()
    dbg_d = nc.dram_tensor("dbg", [P, 16384], F32, kind="ExternalOutput").ap() \
        if dump else None
    dbg2_d = nc.dram_tensor("dbg2", [P, 16384], F32, kind="ExternalOutput").ap() \
        if dump else None

    k_own = nc.dram_tensor("k_own", [D, TC], BF16, kind="Internal").ap()
    v_own = nc.dram_tensor("v_own", [TC, D], BF16, kind="Internal").ap()
    k_all = nc.dram_tensor("k_all", [4 * D, TC], BF16, kind="Internal").ap()
    v_all = nc.dram_tensor("v_all", [4 * TC, D], BF16, kind="Internal").ap()
    k2o_d = nc.dram_tensor("k2o", [1, NH], F32, kind="Internal").ap()
    k2a_d = nc.dram_tensor("k2a", [4, NH], F32, kind="Internal").ap()
    x1d = nc.dram_tensor("x1d", [TC, D], F32, kind="Internal").ap()
    pa_d = nc.dram_tensor("pa", [TC, D], F32, kind="Internal").ap()
    pb_d = nc.dram_tensor("pb", [TC, D], F32, kind="Internal").ap()

    ctx = contextlib.ExitStack()

    ar1 = ctx.enter_context(nc.sbuf_tensor("ar1", [P, 16384], F32))
    ar2 = ctx.enter_context(nc.sbuf_tensor("ar2", [P, 16384], F32))
    ar3 = ctx.enter_context(nc.sbuf_tensor("ar3", [P, 11264], F32))
    ar4 = ctx.enter_context(nc.sbuf_tensor("ar4", [P, 8192], F32))

    ar1_16 = ar1[:].bitcast(BF16)
    ar2_16 = ar2[:].bitcast(BF16)
    ar3_16 = ar3[:].bitcast(BF16)
    ar4_16 = ar4[:].bitcast(BF16)

    lnx_fm = ar1_16
    qsq_v = ar1_16
    attn_fm = ar1_16
    ln2_fm = ar1_16
    ar1_3d = ar1_16.rearrange("p (k t) -> p k t", t=1024)

    q_fm = ar2_16
    wo_slot = [ar2_16[:, j * 8192:(j + 1) * 8192] for j in range(3)]
    h_fm = ar2_16

    xt = [ar3[:, i * 4096:(i + 1) * 4096] for i in range(2)]
    wslot = [ar3_16[:, j * 4096:(j + 1) * 4096] for j in range(3)]
    kst = [ar3_16[:, 12288 + i * 1024: 12288 + (i + 1) * 1024] for i in range(2)]
    vfm_sb = ar3_16[:, 14336:15360]
    vtm_sb = ar3_16[:, 15360:16384]
    ktile = [ar3_16[:, i * 4096:(i + 1) * 4096] for i in range(2)]
    vtile = [ar3_16[:, 8192 + i * 4096: 8192 + (i + 1) * 4096] for i in range(2)]
    pch = [ar3_16[:, 16384 + i * 1024: 16384 + (i + 1) * 1024] for i in range(4)]
    bc_sb = ar3[:, 10240:11264]
    xs = [ar3[:, i * 512:(i + 1) * 512] for i in range(2)]
    x1s = [ar3[:, 1024 + i * 512: 1024 + (i + 1) * 512] for i in range(2)]
    x1t = [ar3[:, i * 4096:(i + 1) * 4096] for i in range(2)]
    dwslot = [ar3_16[:, i * 9216:(i + 1) * 9216] for i in range(2)]
    prev_sb = [ar3[:, 9216 + i * 512: 9216 + (i + 1) * 512] for i in range(2)]
    outst = [ar3[:, 10240 + i * 512: 10240 + (i + 1) * 512] for i in range(2)]

    scr16 = ar4_16[:, 0:4096]
    lntm = [ar4_16[:, 4096 + i * 4096: 4096 + (i + 1) * 4096] for i in range(2)]
    qrow = ar4[0:1, 2048:3072]
    ksq = ar4_16[:, 0:1024]
    zacc = ar4[:, 0:1024]
    zaccB = ar4[:, 4096:5120]
    zacc16 = ar4_16[:, 2048:3072]
    attnU = [ar4_16[:, 3072 + i * 1024: 3072 + (i + 1) * 1024] for i in range(2)]
    srowL = ar4[0:1, 6144:7168]
    srowR = ar4[0:1, 7168:8192]
    gslot = [ar4_16[:, i * 4096:(i + 1) * 4096] for i in range(2)]
    uslot = [ar4_16[:, 8192 + i * 4096: 8192 + (i + 1) * 4096] for i in range(2)]

    ident = ctx.enter_context(nc.sbuf_tensor("ident", [P, P], BF16))
    ones_col = ctx.enter_context(nc.sbuf_tensor("ones_col", [P, 1], BF16))
    ones_row = ctx.enter_context(nc.sbuf_tensor("ones_row", [1, P], F32))
    sumx = ctx.enter_context(nc.sbuf_tensor("sumx", [P, 1], F32))
    ssq = ctx.enter_context(nc.sbuf_tensor("ssq", [P, 1], F32))
    m128 = ctx.enter_context(nc.sbuf_tensor("m128", [P, 1], F32))
    std128 = ctx.enter_context(nc.sbuf_tensor("std128", [P, 1], F32))
    rstd128 = ctx.enter_context(nc.sbuf_tensor("rstd128", [P, 1], F32))
    gs128 = ctx.enter_context(nc.sbuf_tensor("gs128", [P, 1], F32))
    us128 = ctx.enter_context(nc.sbuf_tensor("us128", [P, 1], F32))
    ds128 = ctx.enter_context(nc.sbuf_tensor("ds128", [P, 1], F32))
    sc_sb = ctx.enter_context(nc.sbuf_tensor("sc_sb", [1, 3], F32))
    qn2max = ctx.enter_context(nc.sbuf_tensor("qn2max", [1, NH], F32))
    k2row = ar4[0:1, 6144:6272]
    k2all = ar4[0:1, 6272:6304]
    prodv = ar4[0:1, 6304:6336]
    biasv = ar4[0:1, 6336:6368]
    k2own = ar4[0:1, 6368:6400]
    bias_all = ar4[:, 2560:2592]
    sg_sb = ctx.enter_context(nc.sbuf_tensor("sg_sb", [P, 1024], BF16))
    eps128 = ctx.enter_context(nc.sbuf_tensor("eps128", [P, 1], F32))

    psh = ctx.enter_context(nc.psum_tensor("ps", [P, 4096], F32))
    psf = psh[:]
    ps16 = psf.bitcast(BF16)

    def bank(k):
        return psf[:, k * 512:(k + 1) * 512]

    def pair(k):
        return psf[:, k * 512:(k + 2) * 512]

    tr_p = [ps16[:, i * 1024:(i + 1) * 1024] for i in range(2)]
    ps_qn = psf[0:1, 2560:3584]
    ps_kn = [psf[0:1, 2048 + i * 512: 2048 + (i + 1) * 512] for i in range(2)]
    ps_sums = psf[0:1, 3072:4096]
    ps_bias = psf[:, 3584:3584 + NH]
    ps_scb = [psf[:, 3584 + j: 3585 + j] for j in range(3)]
    ps_vt = ps16[:, 4096:5120]

    all_sems = []

    def mksem(name):
        s = Sem(ctx.enter_context(nc.semaphore(name)))
        all_sems.append(s)
        return s

    def mkstream(name, n):
        return [mksem(f"{name}{i}") for i in range(n)]

    PE = mksem("PE")
    DV = mksem("DV")
    AC = mksem("AC")
    CC = mksem("CC")
    PL = mksem("PL")
    sSC = mksem("sSC")
    sX = mkstream("sX", 2)
    sW = mkstream("sW", 3)
    sKST = mkstream("sKST", 2)
    sVOWN = mksem("sVOWN")
    sK2 = mksem("sK2")
    sKL = mkstream("sKL", 2)
    sVL = mkstream("sVL", 2)
    sWO = mkstream("sWO", 3)
    sXS = mkstream("sXS", 2)
    sX1W = mkstream("sX1W", 2)
    sX1R = mkstream("sX1R", 2)
    sG = mkstream("sG", 2)
    sU = mkstream("sU", 2)
    sDW = mkstream("sDW", 2)
    sPREV = mkstream("sPREV", 2)
    sOUT = mkstream("sOUT", 2)
    sDBG = mksem("sDBG")

    steps = {e: [] for e in ENGS}
    waited = {e: {} for e in ENGS}

    def st(eng, fn, waits=(), inc=None, inck=1):
        w = []
        for (s, v) in waits:
            if not v or v <= 0:
                continue
            if waited[eng].get(s, 0) >= v:
                continue
            waited[eng][s] = v
            w.append((s, v))
        w = tuple(w)
        val = inc.bump(inck) if inc is not None else None
        steps[eng].append((w, fn, (inc.h if inc else None), inck))
        return val

    def dma(eng, fn, sem, waits=()):
        v = st(eng, fn, waits=waits, inc=sem, inck=16)
        return (sem, v)

    marks = {}

    def mark(name):
        marks[name] = {e: len(steps[e]) for e in ENGS}

    def maybe_dump(name, src_ap, second=None):
        if dump != name:
            return
        waits = [(S, S.n) for S in all_sems]

        def f(s, src_ap=src_ap, second=second):
            for i in range(8):
                s.dma_start(out=dbg_d[:, i * 2048:(i + 1) * 2048],
                            in_=src_ap[:, i * 2048:(i + 1) * 2048]
                            ).then_inc(sDBG.h, 16)
            if second is not None:
                for i in range(8):
                    s.dma_start(out=dbg2_d[:, i * 2048:(i + 1) * 2048],
                                in_=second[:, i * 2048:(i + 1) * 2048]
                                ).then_inc(sDBG.h, 16)
            return None
        st("sync", f, waits=waits)
        sDBG.bump(256 if second is not None else 128)

    pl_id0 = st("gpsimd", lambda g: g.memset(ident[:], 0.0), inc=PL)
    st("gpsimd",
       lambda g: g.affine_select(out=ident[:], in_=ident[:],
                                 compare_op=ALU.not_equal, fill=1.0, base=0,
                                 pattern=[[-1, P]], channel_multiplier=1),
       waits=((PL, pl_id0),), inc=PL)

    def setup_pool(g):
        g.memset(ones_col[:], 1.0)
        g.memset(eps128[:], EPS)
        return g.memset(ones_row[:], 1.0)
    pl_setup = st("gpsimd", setup_pool, inc=PL)

    t_sc = dma("sync", lambda s: s.dma_start(out=sc_sb[:], in_=sc_d[:]), sSC)

    def setup_bcast(t):
        last = None
        for j in range(3):
            last = t.matmul(ps_scb[j], ones_row[:], sc_sb[0:1, j:j + 1],
                            start=True, stop=True, skip_group_check=True)
        return last
    pe_scb = st("tensor", setup_bcast,
                waits=((PL, pl_setup), t_sc), inc=PE)

    def setup_copy(v):
        v.tensor_copy(out=gs128[:], in_=ps_scb[0])
        v.tensor_copy(out=us128[:], in_=ps_scb[1])
        return v.tensor_copy(out=ds128[:], in_=ps_scb[2])
    dv_scales = st("vector", setup_copy, waits=((PE, pe_scb),), inc=DV)

    mark("setup")

    L = {"pe_tr": {}, "dv_evac": {}, "dv_apply": {}, "ac_sq": {}}

    def emit_ln_tile(tag, i, src_tile, act_waits=()):
        sl = i % 2
        ac1 = st("scalar",
                 lambda sc, src=src_tile: sc.activation(
                     out=scr16, in_=src, func=AF.Copy, accum_out=sumx[:]),
                 waits=act_waits, inc=AC)
        dv1 = st("vector",
                 lambda v: v.tensor_scalar(out=m128[:], in0=sumx[:],
                                           scalar1=-1.0 / D, scalar2=None,
                                           op0=ALU.mult),
                 waits=((AC, ac1),), inc=DV)
        ac2a = st("scalar",
                  lambda sc, src=src_tile: sc.activation(
                      out=scr16, in_=src, func=AF.Square,
                      bias=m128[:], scale=1.0, accum_out=ssq[:]),
                  waits=((DV, dv1),), inc=AC)
        ac2 = st("scalar",
                 lambda sc: sc.activation(out=std128[:], in_=ssq[:],
                                          func=AF.Sqrt, scale=1.0 / D,
                                          bias=eps128[:]),
                 waits=((AC, ac2a),), inc=AC)
        L["ac_sq"][(tag, i)] = ac2
        dv2 = st("vector", lambda v: v.reciprocal(out=rstd128[:], in_=std128[:]),
                 waits=((AC, ac2),), inc=DV)
        war = L["pe_tr"].get((tag, i - 2, 3), 0)
        dv3 = st("vector",
                 lambda v, src=src_tile, sl=sl: v.tensor_scalar(
                     out=lntm[sl], in0=src, scalar1=m128[:], scalar2=rstd128[:],
                     op0=ALU.add, op1=ALU.mult),
                 waits=((DV, dv2), (PE, war)), inc=DV)
        L["dv_apply"][(tag, i)] = dv3
        return dv3

    def emit_ln_transposes(tag, i):
        sl = i % 2
        dv3 = L["dv_apply"][(tag, i)]
        for b in range(4):
            par = b % 2
            war = max(L["dv_evac"].get((tag, i, b - 2), 0),
                      L["dv_evac"].get((tag, i - 1, b + 2), 0) if b < 2 else 0)

            def tr(t, b=b, par=par, sl=sl):
                last = None
                for j in range(8):
                    k = b * 8 + j
                    last = t.transpose(tr_p[par][:, j * 128:(j + 1) * 128],
                                       lntm[sl][:, k * 128:(k + 1) * 128],
                                       ident[:])
                return last
            pe1 = st("tensor", tr,
                     waits=((DV, dv3), (DV, war), (PL, pl_setup)), inc=PE)
            L["pe_tr"][(tag, i, b)] = pe1

            def evac(v, b=b, par=par, i=i):
                src = tr_p[par].rearrange("p (j q) -> p j q", q=128)
                dst = ar1_3d[:, b * 8:(b + 1) * 8, i * 128:(i + 1) * 128]
                return v.tensor_copy(out=dst, in_=src)
            L["dv_evac"][(tag, i, b)] = st("vector", evac,
                                           waits=((PE, pe1),), inc=DV)
        return L["dv_evac"][(tag, i, 3)]

    t_x = {}
    for i in range(NT):
        sl = i % 2
        war = L["dv_apply"].get(("A", i - 2), 0)
        t_x[i] = dma("sync",
                     lambda s, i=i, sl=sl: s.dma_start(
                         out=xt[sl], in_=x_d[i * P:(i + 1) * P, :]),
                     sX[sl], waits=((DV, war),))
        emit_ln_tile("A", i, xt[sl], act_waits=(t_x[i],))
        emit_ln_transposes("A", i)
    dv_lnx_done = L["dv_evac"][("A", NT - 1, 3)]
    maybe_dump("A", ar1[:])
    mark("A")

    t_w = {}
    pe_mm_done = {}
    dv_kst = {}
    t_kown = {}
    dv_vfm = {}
    pe_vt = {}
    dv_vtm = {}
    t_vown = {}
    dv_q = {}

    def w_dma(src, oc, slot):
        def f(g, src=src, oc=oc, slot=slot):
            return g.dma_start(
                out=wslot[slot].rearrange("p (dc ow) -> p dc ow", ow=P),
                in_=src[oc])
        return f

    def proj_matmuls(t, slot, dst_pair):
        last = None
        for dc in range(DC):
            lhsT = wslot[slot][:, dc * 128:(dc + 1) * 128]
            for half in range(2):
                last = t.matmul(
                    dst_pair[:, half * 512:(half + 1) * 512],
                    lhsT,
                    lnx_fm[:, dc * 1024 + half * 512: dc * 1024 + (half + 1) * 512],
                    start=(dc == 0), stop=(dc == DC - 1),
                    skip_group_check=True)
        return last

    ac_ksq = {}
    kn_pe = {}
    dv_k2 = {}

    def emit_kn(oc):
        def kn_mm(t):
            last = None
            for j in range(2):
                last = t.matmul(ps_kn[j], ones_col[:],
                                ksq[:, j * 512:(j + 1) * 512],
                                start=True, stop=True, skip_group_check=True)
            return last
        kn_pe[oc] = st("tensor", kn_mm,
                       waits=((AC, ac_ksq[oc]), (DV, dv_k2.get(oc - 1, 0))),
                       inc=PE)
        dv_k2[oc] = st("vector",
                       lambda v, oc=oc: v.tensor_reduce(
                           out=k2own[0:1, oc:oc + 1], in_=psf[0:1, 2048:3072],
                           axis=AX.X, op=ALU.max),
                       waits=((PE, kn_pe[oc]),), inc=DV)

    for oc in range(NH):
        widx = oc
        slot = widx % 3
        kw = [(PE, pe_mm_done.get(widx - 3, 0))]
        if widx < 3:
            kw.append((DV, L["dv_apply"][("A", NT - 1)]))
            kw.append((AC, L["ac_sq"][("A", NT - 1)]))
        t_w[widx] = dma("gpsimd", w_dma(wk_d, oc, slot), sW[slot], waits=kw)
        pe_mm_done[widx] = st(
            "tensor",
            lambda t, slot=slot, oc=oc: proj_matmuls(t, slot, pair((oc % 2) * 2)),
            waits=(t_w[widx], (DV, dv_lnx_done), (DV, dv_kst.get(oc - 2, 0))),
            inc=PE)
        if oc >= 1:
            emit_kn(oc - 1)
        dv_kst[oc] = st(
            "vector",
            lambda v, oc=oc: v.tensor_copy(out=kst[oc % 2], in_=pair((oc % 2) * 2)),
            waits=((PE, pe_mm_done[widx]),
                   t_kown.get(oc - 2, (CC, 0)),
                   (AC, ac_ksq.get(oc - 2, 0))), inc=DV)
        t_kown[oc] = dma(
            "sync",
            lambda s, oc=oc: s.dma_start(
                out=k_own[oc * 128:(oc + 1) * 128, :], in_=kst[oc % 2]),
            sKST[oc % 2], waits=((DV, dv_kst[oc]),))
        ac_ksq[oc] = st("scalar",
                        lambda sc, oc=oc: sc.activation(
                            out=ksq, in_=kst[oc % 2], func=AF.Square),
                        waits=((DV, dv_kst[oc]), (PE, kn_pe.get(oc - 1, 0))),
                        inc=AC)
    emit_kn(NH - 1)
    t_k2w = dma("sync",
                lambda s: s.dma_start(out=k2o_d[:], in_=k2own[0:1, :]),
                sK2, waits=((DV, dv_k2[NH - 1]),))

    rg = [[0, 1, 2, 3], [4, 5, 6, 7]]
    cc_k2 = cc_k = None
    for oc in range(NH):
        widx = NH + oc
        slot = widx % 3
        t_w[widx] = dma("gpsimd", w_dma(wv_d, oc, slot), sW[slot],
                        waits=((PE, pe_mm_done.get(widx - 3, 0)),))
        if oc == 16:
            cc_k2 = st("gpsimd",
                       lambda g: g.collective_compute(
                           "AllGather", ALU.bypass, replica_groups=rg,
                           ins=[k2o_d.opt()], outs=[k2a_d.opt()]),
                       waits=(t_k2w,), inc=CC)
            cc_k = st("gpsimd",
                      lambda g: g.collective_compute(
                          "AllGather", ALU.bypass, replica_groups=rg,
                          ins=[k_own.opt()], outs=[k_all.opt()]),
                      waits=((sKST[0], sKST[0].n), (sKST[1], sKST[1].n)),
                      inc=CC)
        mm_w = [t_w[widx], (DV, dv_vfm.get(oc - 2, 0))]
        if oc < 2:
            mm_w.append((DV, dv_kst[NH - 1]))
        pe_mm_done[widx] = st(
            "tensor",
            lambda t, slot=slot, oc=oc: proj_matmuls(t, slot, pair((oc % 2) * 2)),
            waits=mm_w, inc=PE)
        dv_vfm[oc] = st(
            "vector",
            lambda v, oc=oc: v.tensor_copy(out=vfm_sb, in_=pair((oc % 2) * 2)),
            waits=((PE, pe_mm_done[widx]), (PE, pe_vt.get(oc - 1, 0))), inc=DV)

        def vtr(t, oc=oc):
            last = None
            for j in range(8):
                last = t.transpose(ps_vt[:, j * 128:(j + 1) * 128],
                                   vfm_sb[:, j * 128:(j + 1) * 128], ident[:])
            return last
        pe_vt[oc] = st("tensor", vtr,
                       waits=((DV, dv_vfm[oc]), (DV, dv_vtm.get(oc - 1, 0))),
                       inc=PE)
        dv_vtm[oc] = st(
            "vector",
            lambda v: v.tensor_copy(out=vtm_sb, in_=ps_vt),
            waits=((PE, pe_vt[oc]), t_vown.get(oc - 1, (CC, 0))), inc=DV)
        t_vown[oc] = dma(
            "sync",
            lambda s, oc=oc: s.dma_start(
                out=v_own[:, oc * 128:(oc + 1) * 128].rearrange(
                    "(b p) o -> p b o", p=P),
                in_=vtm_sb.rearrange("p (b o) -> p b o", o=P)),
            sVOWN, waits=((DV, dv_vtm[oc]),))

    cc_v = st("gpsimd",
              lambda g: g.collective_compute(
                  "AllGather", ALU.bypass, replica_groups=rg,
                  ins=[v_own.opt()], outs=[v_all.opt()]),
              waits=((sVOWN, sVOWN.n),), inc=CC)

    for oc in range(NH):
        widx = 2 * NH + oc
        slot = widx % 3
        t_w[widx] = dma("gpsimd", w_dma(wq_d, oc, slot), sW[slot],
                        waits=((PE, pe_mm_done.get(widx - 3, 0)),))
        mm_w = [t_w[widx], (DV, dv_q.get(oc - 2, 0))]
        if oc < 2:
            mm_w.append((DV, dv_vfm[NH - 1]))
        pe_mm_done[widx] = st(
            "tensor",
            lambda t, slot=slot, oc=oc: proj_matmuls(t, slot, pair((oc % 2) * 2)),
            waits=mm_w, inc=PE)
        dv_q[oc] = st(
            "vector",
            lambda v, oc=oc: v.tensor_copy(
                out=q_fm[:, oc * 1024:(oc + 1) * 1024], in_=pair((oc % 2) * 2)),
            waits=((PE, pe_mm_done[widx]),), inc=DV)
    pe_q_done = pe_mm_done[3 * NH - 1]

    ac_qsq = st("scalar",
                lambda sc: sc.activation(out=qsq_v, in_=q_fm, func=AF.Square),
                waits=((DV, dv_q[NH - 1]),), inc=AC)
    pe_qn = {}
    dv_qrow = {}
    dv_qn = {}
    for h in range(NH):
        def qn_mm(t, h=h):
            last = None
            for half in range(2):
                last = t.matmul(
                    ps_qn[:, half * 512:(half + 1) * 512], ones_col[:],
                    qsq_v[:, h * 1024 + half * 512: h * 1024 + (half + 1) * 512],
                    start=True, stop=True, skip_group_check=True)
            return last
        pe_qn[h] = st("tensor", qn_mm,
                      waits=((AC, ac_qsq), (DV, dv_qrow.get(h - 1, 0))), inc=PE)
        dv_qrow[h] = st("vector",
                        lambda v: v.tensor_copy(out=qrow, in_=ps_qn),
                        waits=((PE, pe_qn[h]), (DV, dv_qn.get(h - 1, 0))), inc=DV)
        dv_qn[h] = st("vector",
                      lambda v, h=h: v.tensor_reduce(
                          out=qn2max[0:1, h:h + 1], in_=qrow, axis=AX.X,
                          op=ALU.max),
                      waits=((DV, dv_qrow[h]),), inc=DV)

    t_k2r = dma("sync",
                lambda s: s.dma_start(out=k2row,
                                      in_=k2a_d.rearrange("c h -> (c h)")),
                sK2, waits=((CC, cc_k2),))
    dv_k2c = st("vector",
                lambda v: v.tensor_tensor(out=k2all, in0=k2row[0:1, 0:NH],
                                          in1=k2row[0:1, NH:2 * NH],
                                          op=ALU.max),
                waits=(t_k2r,), inc=DV)
    dv_k2c = st("vector",
                lambda v: v.tensor_tensor(out=k2all, in0=k2all,
                                          in1=k2row[0:1, 2 * NH:3 * NH],
                                          op=ALU.max),
                waits=((DV, dv_k2c),), inc=DV)
    dv_k2c = st("vector",
                lambda v: v.tensor_tensor(out=k2all, in0=k2all,
                                          in1=k2row[0:1, 3 * NH:4 * NH],
                                          op=ALU.max),
                waits=((DV, dv_k2c),), inc=DV)
    dv_prod = st("vector",
                 lambda v: v.tensor_tensor(out=prodv, in0=qn2max[:],
                                           in1=k2all, op=ALU.mult),
                 waits=((DV, dv_k2c), (DV, dv_qn[NH - 1])), inc=DV)
    ac_biasv = st("scalar",
                  lambda sc: sc.activation(out=biasv, in_=prodv,
                                           func=AF.Sqrt, scale=1.0 / DK),
                  waits=((DV, dv_prod),), inc=AC)
    pe_bcb = st("tensor",
                lambda t: t.matmul(ps_bias, ones_row[:], biasv,
                                   start=True, stop=True,
                                   skip_group_check=True),
                waits=((AC, ac_biasv), (DV, dv_scales)), inc=PE)
    dv_biasall = st("vector",
                    lambda v: v.tensor_scalar(out=bias_all, in0=ps_bias,
                                              scalar1=-1.0, scalar2=None,
                                              op0=ALU.mult),
                    waits=((PE, pe_bcb),), inc=DV)
    maybe_dump("B", ar2[:], second=ar1[:])
    mark("B")
    mark("Q")

    t_k = {}
    t_v = {}
    dv_attn = {}
    dv_bc = {}
    pe_st_last = {}
    pe_pv_last = {}
    pe_sums_last = {}
    pe_bc = {}
    dv_evacU = {}
    dv_srow = {}
    ac_rec = {}
    ac_exp_l = {}
    dvD = [0]
    plD = [PL.n]

    t_wo = {}

    def emit_wo(widx, extra_waits=()):
        os_, half = widx // 2, widx % 2
        slot = widx % 3
        t_wo[widx] = dma(
            "gpsimd",
            lambda g, os_=os_, half=half, slot=slot: g.dma_start(
                out=wo_slot[slot].rearrange("p (oc ow) -> p oc ow", ow=512),
                in_=wo_d[os_, :, half * 16:(half + 1) * 16, :]),
            sWO[slot], waits=extra_waits)

    def dvst(fn, waits=(), ):
        v = st("vector", fn, waits=tuple(waits) + ((DV, dvD[0]),), inc=DV)
        dvD[0] = v
        return v

    def emit_psums(cp, h, psums, exp_ac):
        w = [(AC, exp_ac[cp])]
        if cp == 3:
            if h == 0:
                w += [(DV, dv_qrow[NH - 1]), (DV, dv_qn[NH - 1]),
                      (DV, dv_biasall)]
            else:
                w.append((DV, dv_bc[h - 1]))

        def sums_mm(t, cp=cp):
            last = None
            for j in range(2):
                last = t.matmul(ps_sums[:, j * 512:(j + 1) * 512], ones_col[:],
                                pch[cp % 4][:, j * 512:(j + 1) * 512],
                                start=(cp == 3), stop=False,
                                skip_group_check=True)
            return last
        psums[cp] = st("tensor", sums_mm, waits=w, inc=PE)

    def emit_merge(h):
        def merge_mm(t):
            last = None
            for j in range(2):
                last = t.matmul(ps_sums[:, j * 512:(j + 1) * 512], ones_col[:],
                                zacc16[:, j * 512:(j + 1) * 512],
                                start=False, stop=True, skip_group_check=True)
            return last
        pe_sums_last[h] = st("tensor", merge_mm,
                             waits=((DV, zcast[h]),), inc=PE)

    def emit_tail_dv(h):
        dv_bc[h] = dvst(lambda v: v.tensor_copy(out=bc_sb, in_=psf[:, 3072:4096]),
                        waits=((PE, pe_bc[h]),))
        dv_attn[h] = dvst(
            lambda v, h=h: v.tensor_tensor(
                out=attn_fm[:, h * 1024:(h + 1) * 1024],
                in0=attnU[h % 2], in1=bc_sb, op=ALU.mult))

    def emit_tail_ac(h):
        st("scalar", lambda sc: sc.activation(out=srowR, in_=srowL, func=AF.Ln),
           waits=((DV, dv_srow[h]),), inc=AC)
        ac_rec[h] = st("scalar",
                       lambda sc: sc.activation(out=srowL, in_=srowR,
                                                func=AF.Exp, scale=-1.0),
                       inc=AC)

    def emit_bc(h):
        def bc_mm(t):
            last = None
            for j in range(2):
                last = t.matmul(psf[:, 3072 + j * 512: 3072 + (j + 1) * 512],
                                ones_row[:], srowL[0:1, j * 512:(j + 1) * 512],
                                start=True, stop=True, skip_group_check=True)
            return last
        pe_bc[h] = st("tensor", bc_mm,
                      waits=((AC, ac_rec[h]), (DV, dv_srow[h])), inc=PE)

    zcast = {}
    for h in range(NH):
        sl = h % 2

        def kload(s, h=h, sl=sl):
            for j in range(4):
                s.dma_start(
                    out=ktile[sl][:, j * 1024:(j + 1) * 1024],
                    in_=k_all[j * D + h * 128: j * D + (h + 1) * 128, :]
                ).then_inc(sKL[sl].h, 16)
            return None
        klw = [(CC, cc_k), (PE, pe_st_last.get(h - 2, 0))]
        if h < 2:
            klw.append((PE, pe_q_done))
        st("sync", kload, waits=klw)
        t_k[h] = (sKL[sl], sKL[sl].bump(64))

        def vload(s, h=h, sl=sl):
            for half in range(2):
                s.dma_start(
                    out=vtile[sl][:, half * 2048:(half + 1) * 2048].rearrange(
                        "p (kc dv) -> p kc dv", dv=128),
                    in_=v_all[half * 2048:(half + 1) * 2048,
                              h * 128:(h + 1) * 128].rearrange(
                        "(kc p) dv -> p kc dv", p=P)
                ).then_inc(sVL[sl].h, 16)
            return None
        vlw = [(CC, cc_v), (PE, pe_pv_last.get(h - 2, 0))]
        if h < 2:
            vlw.append((PE, pe_q_done))
            vlw.append((DV, dv_vtm[NH - 1]))
            vlw.append((AC, ac_ksq[NH - 1]))
        st("sync", vload, waits=vlw)
        t_v[h] = (sVL[sl], sVL[sl].bump(32))

        st_pe = {}
        exp_ac = {}
        pv_pe = {}
        zadd = {}
        prev_exp = ac_exp_l.get(h - 1, {})

        def emit_st(c):
            w = [t_k[h]]
            if c >= 2:
                w.append((AC, exp_ac[c - 2]))
            elif h > 0:
                w.append((AC, prev_exp[30 + c]))
            elif c == 0:
                w.append((DV, dv_q[NH - 1]))

            def st_mm(t, c=c, sl=sl, h=h):
                last = None
                for half in range(2):
                    last = t.matmul(
                        pair((c % 2) * 2)[:, half * 512:(half + 1) * 512],
                        ktile[sl][:, c * 128:(c + 1) * 128],
                        q_fm[:, h * 1024 + half * 512: h * 1024 + (half + 1) * 512],
                        start=True, stop=True, skip_group_check=True)
                return last
            st_pe[c] = st("tensor", st_mm, waits=w, inc=PE)

        def emit_pv(c):
            w = [(AC, exp_ac[c]), t_v[h]]
            if c == 0:
                if h > 0:
                    w.append((DV, dv_evacU[h - 1]))
                else:
                    w += [(DV, dv_vtm[NH - 1]), (DV, dv_k2[NH - 1]),
                          (DV, dv_qrow[NH - 1])]

            def pv_mm(t, c=c, sl=sl):
                last = None
                for half in range(2):
                    last = t.matmul(
                        pair(4)[:, half * 512:(half + 1) * 512],
                        vtile[sl][:, c * 128:(c + 1) * 128],
                        pch[c % 4][:, half * 512:(half + 1) * 512],
                        start=(c == 0), stop=(c == KC - 1),
                        skip_group_check=True)
                return last
            pv_pe[c] = st("tensor", pv_mm, waits=w, inc=PE)

        psums = {}
        dvadd = {}
        for c in range(KC):
            if c >= 2:
                emit_pv(c - 2)
            emit_st(c)
            if c == 3 and h > 0:
                emit_merge(h - 1)
            if c == 6 and h > 0:
                emit_bc(h - 1)
            if c >= 7 and c % 4 == 3:
                emit_psums(c - 4, h, psums, exp_ac)

            if c == 4 and h > 0:
                emit_tail_ac(h - 1)
            w = [(PE, st_pe[c])]
            if c >= 4:
                prev = c - 4
                if prev % 4 == 3:
                    w.append((PE, psums[prev]))
                else:
                    w.append((DV, dvadd[prev]))
            elif h > 0:
                w.append((PE, pe_pv_last[h - 1]))
                w.append((DV, zcast[h - 1]))
            if h == 0 and c == 0:
                w.append((DV, dv_biasall))
            exp_ac[c] = st("scalar",
                           lambda sc, c=c, h=h: sc.activation(
                               out=pch[c % 4], in_=pair((c % 2) * 2),
                               func=AF.Exp, bias=bias_all[:, h:h + 1],
                               scale=INV_SQRT_DK),
                           waits=w, inc=AC)

            if c == 3 and h > 0:
                dv_srow[h - 1] = dvst(
                    lambda v: v.tensor_copy(out=srowL, in_=ps_sums),
                    waits=((PE, pe_sums_last[h - 1]),))
            if c == 6 and h > 0:
                emit_tail_dv(h - 1)
            if c % 4 != 3:
                acc = zaccB if c % 4 == 1 else zacc
                if c <= 1:
                    dvadd[c] = dvst(
                        lambda v, c=c, acc=acc: v.tensor_copy(
                            out=acc, in_=pch[c % 4]),
                        waits=((AC, exp_ac[c]),))
                else:
                    dvadd[c] = dvst(
                        lambda v, c=c, acc=acc: v.tensor_tensor(
                            out=acc, in0=acc, in1=pch[c % 4], op=ALU.add),
                        waits=((AC, exp_ac[c]),))

        emit_pv(KC - 2)
        emit_pv(KC - 1)
        emit_psums(KC - 1, h, psums, exp_ac)
        pe_st_last[h] = st_pe[KC - 1]
        pe_pv_last[h] = pv_pe[KC - 1]
        ac_exp_l[h] = exp_ac

        zcast[h] = dvst(
            lambda v: v.tensor_tensor(out=zacc16, in0=zacc, in1=zaccB,
                                      op=ALU.add),
            waits=((PE, pe_sums_last.get(h - 1, 0)),))
        dv_evacU[h] = dvst(
            lambda v, h=h: v.tensor_copy(out=attnU[h % 2], in_=pair(4)),
            waits=((PE, pe_pv_last[h]),))
        if h in (7, 15, 23):
            emit_wo((h - 7) // 8, extra_waits=((PE, pe_st_last[h]),))

    emit_merge(NH - 1)
    dv_srow[NH - 1] = dvst(
        lambda v: v.tensor_copy(out=srowL, in_=ps_sums),
        waits=((PE, pe_sums_last[NH - 1]),))
    emit_tail_ac(NH - 1)
    emit_bc(NH - 1)
    emit_tail_dv(NH - 1)

    ac_rec_last = AC.n
    maybe_dump("D", ar1[:])
    mark("D")

    pe_o = {}
    t_xs = {}
    dv_x1 = {}
    t_x1w = {}

    peE0 = PE.n
    for widx in range(3, 2 * OS):
        war = peE0 + ((widx - 3) // 2 + 1) * NT
        emit_wo(widx, extra_waits=((PE, war),))

    def eprev(os_, tc, k):
        i = os_ * NT + tc - k
        return (i // NT, i % NT) if i >= 0 else None

    for os_ in range(OS):
        for tc in range(NT):
            par = (os_ * NT + tc) % 2
            p2 = eprev(os_, tc, 2)
            w = [t_wo[os_ * 2], t_wo[os_ * 2 + 1],
                 (DV, dv_attn[NH - 1]),
                 (DV, dv_x1.get(p2, 0))]

            def o_mm(t, os_=os_, tc=tc, par=par):
                last = None
                for oc in range(NH):
                    slot = (os_ * 2 + oc // 16) % 3
                    last = t.matmul(
                        bank(par),
                        attn_fm[:, oc * 1024 + tc * 128: oc * 1024 + (tc + 1) * 128],
                        wo_slot[slot][:, (oc % 16) * 512:((oc % 16) + 1) * 512],
                        start=(oc == 0), stop=(oc == NH - 1),
                        skip_group_check=True)
                return last
            pe_o[(os_, tc)] = st("tensor", o_mm, waits=w, inc=PE)
            assert pe_o[(os_, tc)] == peE0 + os_ * NT + tc + 1

            xw = [(DV, dv_x1.get(p2, 0))]
            if os_ * NT + tc < 2:
                xw.append((PE, pe_sums_last[NH - 1]))
                xw.append((DV, dv_attn[NH - 1]))
            t_xs[(os_, tc)] = dma(
                "sync",
                lambda s, os_=os_, tc=tc, par=par: s.dma_start(
                    out=xs[par],
                    in_=x_d[tc * 128:(tc + 1) * 128, os_ * 512:(os_ + 1) * 512]),
                sXS[par], waits=xw)
            dv_x1[(os_, tc)] = st(
                "vector",
                lambda v, par=par: v.tensor_tensor(
                    out=x1s[par], in0=bank(par), in1=xs[par], op=ALU.add),
                waits=((PE, pe_o[(os_, tc)]), t_xs[(os_, tc)],
                       t_x1w.get(p2, (CC, 0))), inc=DV)
            t_x1w[(os_, tc)] = dma(
                "sync",
                lambda s, os_=os_, tc=tc, par=par: s.dma_start(
                    out=x1d[tc * 128:(tc + 1) * 128, os_ * 512:(os_ + 1) * 512],
                    in_=x1s[par]),
                sX1W[par], waits=((DV, dv_x1[(os_, tc)]),))

    pe_attn_free = pe_o[(OS - 1, NT - 1)]
    mark("E")

    t_x1r = {}
    for i in range(NT):
        sl = i % 2
        rw = [(sX1W[0], sX1W[0].n), (sX1W[1], sX1W[1].n),
              (DV, L["dv_apply"].get(("E2", i - 2), 0))]
        t_x1r[i] = dma("sync",
                       lambda s, i=i, sl=sl: s.dma_start(
                           out=x1t[sl], in_=x1d[i * P:(i + 1) * P, :]),
                       sX1R[sl], waits=rw)
        extra = [t_x1r[i]]
        if i == 0:
            extra += [(PE, pe_attn_free), (AC, ac_rec_last)]
        emit_ln_tile("E2", i, x1t[sl], act_waits=tuple(extra))
        emit_ln_transposes("E2", i)
    dv_ln2_done = L["dv_evac"][("E2", NT - 1, 3)]
    ac_e2_last = L["ac_sq"][("E2", NT - 1)]
    maybe_dump("E2", ar1[:])
    mark("E2")

    t_g = {}
    t_u = {}
    pe_gate = {}
    pe_up = {}
    ac_silu = {}
    dv_h = {}
    pe_dmm = {}
    dv_out = {}
    t_outw = {}
    t_prev = {}

    t_dw_d = {}

    def emit_dw(g_i, gsz, os_):
        g0 = sum(GROUPS[:g_i])
        gp = g_i * OS + os_ - 2
        dwar = pe_dmm.get((gp // OS, gp % OS, NT - 1), 0) if gp >= 0 else 0
        if g_i == 0 and os_ < 2:
            dww = [(PE, pe_attn_free), (DV, dv_ln2_done),
                   (AC, ac_e2_last)]
        else:
            dww = [(PE, dwar)]
        hsz = gsz // 2

        def dwload(g, os_=os_, g0=g0, gsz=gsz, hsz=hsz):
            sem = sDW[os_ % 2].h
            g.dma_start(
                out=dwslot[os_ % 2][:, 0:hsz * 512].rearrange(
                    "p (ic ow) -> p ic ow", ow=512),
                in_=dw_d[os_, :, g0:g0 + hsz, :]).then_inc(sem, 16)
            g.dma_start(
                out=dwslot[os_ % 2][:, hsz * 512:gsz * 512].rearrange(
                    "p (ic ow) -> p ic ow", ow=512),
                in_=dw_d[os_, :, g0 + hsz:g0 + gsz, :]).then_inc(sem, 16)
            return None
        st("gpsimd", dwload, waits=dww)
        t_dw_d[(g_i, os_)] = (sDW[os_ % 2], sDW[os_ % 2].bump(32))

    def fprev(g_i, os_, tc, k):
        i = (g_i * OS + os_) * NT + tc - k
        if i < 0:
            return None
        t = i % NT
        i //= NT
        return (i // OS, i % OS, t)

    gbase = 0
    for g_i, gsz in enumerate(GROUPS):
        ic_last = gbase + gsz - 1
        for icg in range(gsz):
            ic = gbase + icg
            sl2 = ic % 2
            gw_w = [(PE, pe_gate.get(ic - 2, 0))]
            uw_w = [(PE, pe_up.get(ic - 2, 0))]
            if ic == 0:
                gw_w += [(DV, dv_ln2_done), (AC, ac_e2_last)]
                uw_w += [(DV, dv_ln2_done), (AC, ac_e2_last)]
            t_g[ic] = dma("gpsimd",
                          lambda g, ic=ic, sl2=sl2: g.dma_start(
                              out=gslot[sl2].rearrange("p (dc iw) -> p dc iw", iw=P),
                              in_=gw_d[ic]),
                          sG[sl2], waits=gw_w)
            t_u[ic] = dma("gpsimd",
                          lambda g, ic=ic, sl2=sl2: g.dma_start(
                              out=uslot[sl2].rearrange("p (dc iw) -> p dc iw", iw=P),
                              in_=uw_d[ic]),
                          sU[sl2], waits=uw_w)
            if icg >= gsz - 2:
                emit_dw(g_i, gsz, icg - (gsz - 2))

            def gate_mm(t, ic=ic, sl2=sl2):
                last = None
                dst = pair((ic % 2) * 2)
                for dc in range(DC):
                    lhsT = gslot[sl2][:, dc * 128:(dc + 1) * 128]
                    for half in range(2):
                        last = t.matmul(
                            dst[:, half * 512:(half + 1) * 512], lhsT,
                            ln2_fm[:, dc * 1024 + half * 512:
                                   dc * 1024 + (half + 1) * 512],
                            start=(dc == 0), stop=(dc == DC - 1),
                            skip_group_check=True)
                return last
            gmw = [t_g[ic], (DV, dv_ln2_done), (AC, ac_silu.get(ic - 2, 0))]
            if icg < 2 and g_i > 0:
                gmw.append((DV, dv_out[(g_i - 1, OS - 1, NT - 1)]))
            pe_gate[ic] = st("tensor", gate_mm, waits=gmw, inc=PE)

            def up_mm(t, ic=ic, sl2=sl2):
                last = None
                dst = pair(4 + (ic % 2) * 2)
                for dc in range(DC):
                    lhsT = uslot[sl2][:, dc * 128:(dc + 1) * 128]
                    for half in range(2):
                        last = t.matmul(
                            dst[:, half * 512:(half + 1) * 512], lhsT,
                            ln2_fm[:, dc * 1024 + half * 512:
                                   dc * 1024 + (half + 1) * 512],
                            start=(dc == 0), stop=(dc == DC - 1),
                            skip_group_check=True)
                return last
            umw = [t_u[ic], (DV, dv_h.get(ic - 2, 0))]
            pe_up[ic] = st("tensor", up_mm, waits=umw, inc=PE)

            ac_silu[ic] = st("scalar",
                             lambda sc, ic=ic: sc.activation(
                                 out=sg_sb[:], in_=pair((ic % 2) * 2),
                                 func=AF.Silu, scale=gs128[:]),
                             waits=((PE, pe_gate[ic]), (DV, dv_h.get(ic - 1, 0)),
                                    (DV, dv_scales)),
                             inc=AC)
            dv_h[ic] = st("vector",
                          lambda v, ic=ic, icg=icg: v.scalar_tensor_tensor(
                              out=h_fm[:, icg * 1024:(icg + 1) * 1024],
                              in0=pair(4 + (ic % 2) * 2), scalar=us128[:],
                              in1=sg_sb[:], op0=ALU.mult, op1=ALU.mult),
                          waits=((AC, ac_silu[ic]), (PE, pe_up[ic])), inc=DV)

        src_part = [x1d, pa_d, pb_d, pa_d, pb_d][g_i]
        dst_part = [pa_d, pb_d, pa_d, pb_d, out_d][g_i]
        for os_ in range(OS):
            if os_ >= 2:
                emit_dw(g_i, gsz, os_)
            t_dw = t_dw_d[(g_i, os_)]
            for tc in range(NT):
                par = ((g_i * OS + os_) * NT + tc) % 2
                p2 = fprev(g_i, os_, tc, 2)
                w = [t_dw, (DV, dv_out.get(p2, 0))]
                if g_i == 0 and os_ == 0 and tc < 2:
                    w.append((AC, ac_silu[ic_last]))

                def d_mm(t, os_=os_, tc=tc, par=par, gsz=gsz):
                    last = None
                    for icg in range(gsz):
                        last = t.matmul(
                            bank(par),
                            h_fm[:, icg * 1024 + tc * 128:
                                 icg * 1024 + (tc + 1) * 128],
                            dwslot[os_ % 2][:, icg * 512:(icg + 1) * 512],
                            start=(icg == 0), stop=(icg == gsz - 1),
                            skip_group_check=True)
                    return last
                pe_dmm[(g_i, os_, tc)] = st("tensor", d_mm, waits=w, inc=PE)

                pw = [(DV, dv_out.get(p2, 0))]
                if g_i > 0:
                    pw.append(t_outw[(g_i - 1, os_, tc)])
                if g_i == 0 and os_ == 0 and tc < 2:
                    pw.append((DV, dv_ln2_done))
                t_prev[(g_i, os_, tc)] = dma(
                    "sync",
                    lambda s, os_=os_, tc=tc, par=par, src=src_part: s.dma_start(
                        out=prev_sb[par],
                        in_=src[tc * 128:(tc + 1) * 128,
                                os_ * 512:(os_ + 1) * 512]),
                    sPREV[par], waits=pw)
                dv_out[(g_i, os_, tc)] = st(
                    "vector",
                    lambda v, par=par: v.scalar_tensor_tensor(
                        out=outst[par], in0=bank(par), scalar=ds128[:],
                        in1=prev_sb[par], op0=ALU.mult, op1=ALU.add),
                    waits=((PE, pe_dmm[(g_i, os_, tc)]),
                           t_prev[(g_i, os_, tc)],
                           t_outw.get(p2, (CC, 0))), inc=DV)
                t_outw[(g_i, os_, tc)] = dma(
                    "sync",
                    lambda s, os_=os_, tc=tc, par=par, dst=dst_part: s.dma_start(
                        out=dst[tc * 128:(tc + 1) * 128,
                                os_ * 512:(os_ + 1) * 512],
                        in_=outst[par]),
                    sOUT[par], waits=((DV, dv_out[(g_i, os_, tc)]),))
        gbase += gsz

    st("sync", lambda s: None,
       waits=((sOUT[0], sOUT[0].n), (sOUT[1], sOUT[1].n)))
    mark("F")

    if stop is not None:
        cut = marks[stop]
        for e in ENGS:
            steps[e] = steps[e][:cut[e]]
    with nc.Block() as block:
        def runner(prog):
            def run(eng):
                for (waits, fn, sem_h, inck) in prog:
                    for (s, v) in waits:
                        eng.wait_ge(s.h, v)
                    inst = fn(eng)
                    if sem_h is not None:
                        assert inst is not None
                        inst.then_inc(sem_h, inck)
            return run
        block.sync(runner(steps["sync"]))
        block.tensor(runner(steps["tensor"]))
        block.vector(runner(steps["vector"]))
        block.scalar(runner(steps["scalar"]))
        block.gpsimd(runner(steps["gpsimd"]))

    ctx.close()
    return nc


_NC_CACHE = {}


def _get_nc():
    if "nc" not in _NC_CACHE:
        _NC_CACHE["nc"] = build_program()
    return _NC_CACHE["nc"]


def kernel(L0_attn_x, L0_attn_w_q, L0_attn_w_k, L0_attn_w_v, L0_attn_w_o,
           L0_ffn_gate_w, L0_ffn_up_w, L0_ffn_down_w,
           L0_ffn_gate_scale, L0_ffn_up_scale, L0_ffn_down_scale):
    x = np.ascontiguousarray(np.asarray(L0_attn_x, dtype=np.float32)).reshape(8192, D)

    def stage_lhs(w, nchunk):
        w = np.asarray(w, dtype=np.float32)
        return np.ascontiguousarray(
            w.reshape(nchunk, P, DC, P).transpose(0, 3, 2, 1))

    def stage_rhs(w, nchunk):
        w = np.asarray(w, dtype=np.float32)
        return np.ascontiguousarray(
            w.reshape(OS, 512, nchunk, P).transpose(0, 3, 2, 1))

    wq = stage_lhs(L0_attn_w_q, NH)
    wk = stage_lhs(L0_attn_w_k, NH)
    wv = stage_lhs(L0_attn_w_v, NH)
    wo = stage_rhs(L0_attn_w_o, NH)
    gw = stage_lhs(L0_ffn_gate_w, ICT)
    uw = stage_lhs(L0_ffn_up_w, ICT)
    dw = stage_rhs(L0_ffn_down_w, ICT)
    sc = np.array([[np.float32(L0_ffn_gate_scale),
                    np.float32(L0_ffn_up_scale),
                    np.float32(L0_ffn_down_scale)]], dtype=np.float32)

    nc = _get_nc()
    in_maps = []
    for c in range(8):
        in_maps.append({
            "x": np.ascontiguousarray(x[c * TC:(c + 1) * TC]),
            "wq": wq, "wk": wk, "wv": wv, "wo": wo,
            "gw": gw, "uw": uw, "dw": dw, "sc": sc,
        })
    res = run_bass_kernel_spmd(nc, in_maps, core_ids=list(range(8)))
    out = np.concatenate([res.results[c]["out"] for c in range(8)], axis=0)
    return out.reshape(2, 4096, D)
